# version 04
# baseline (speedup 1.0000x reference)
import numpy as np
import ml_dtypes
import concourse.bass as bass
import concourse.mybir as mybir
from concourse.bass_utils import run_bass_kernel_spmd

F32 = mybir.dt.float32
BF16 = mybir.dt.bfloat16
I32 = mybir.dt.int32
AF = mybir.ActivationFunctionType
ALU = mybir.AluOpType
AX = mybir.AxisListType

D = 2048
S = 8192
NH = 16
DK = 128
MLP = 16384
NBLK = 16
NT = 16
OWN = 2048
EPS = 1e-5
G_C0 = 0.7978845608028654
G_C1 = 0.7978845608028654 * 0.044715
NEGBIG = -30000.0
INVSQ = float(1.0 / np.sqrt(128.0))
TWO_PI_HI = float(np.float32(6.2831855))
TWO_PI_LO = float(2 * np.pi - np.float64(np.float32(6.2831855)))
MAGIC = float(np.float32(3 * 2.0 ** 22))
KB = 1024

ENGINES_EMIT = ("sync", "tensor", "vector", "scalar")


class Builder:
    def __init__(self, sems):
        self.h = sems
        self.cnt = None
        self.active = None
        self.eng = None

    def start(self, active, eng):
        self.cnt = {k: 0 for k in ("L", "T", "P", "V", "A")}
        self.active = active
        self.eng = eng

    def dma(self, out, in_, sem="L"):
        if self.active == "sync":
            self.eng.dma_start(out=out, in_=in_).then_inc(self.h[sem], 16)
        self.cnt[sem] += 16
        return self.cnt[sem]

    def wait(self, engine, sem, val):
        if val <= 0:
            return
        if self.active == engine:
            self.eng.wait_ge(self.h[sem], val)

    def mm(self, out, lhsT, rhs, start, stop, inc=False):
        if self.active == "tensor":
            ins = self.eng.matmul(out, lhsT, rhs, start=start, stop=stop)
            if inc:
                ins.then_inc(self.h["P"], 1)
        if inc:
            self.cnt["P"] += 1
            return self.cnt["P"]
        return None

    def tr(self, out, in_, ident, inc=False):
        if self.active == "tensor":
            ins = self.eng.transpose(out, in_, ident)
            if inc:
                ins.then_inc(self.h["P"], 1)
        if inc:
            self.cnt["P"] += 1
            return self.cnt["P"]
        return None

    def ve(self, op, *a, **k):
        if self.active == "vector":
            getattr(self.eng, op)(*a, **k)

    def vdone(self, n=1):
        if self.active == "vector":
            self.eng.maybe_drain_then_inc((self.h["V"], n))
        self.cnt["V"] += n
        return self.cnt["V"]

    def ac(self, op, *a, **k):
        if self.active == "scalar":
            getattr(self.eng, op)(*a, **k)

    def adone(self, n=1):
        if self.active == "scalar":
            self.eng.maybe_drain_then_inc((self.h["A"], n))
        self.cnt["A"] += n
        return self.cnt["A"]

    def vdrain(self):
        n = self.vdone()
        self.wait("vector", "V", n)


def build_program():
    nc = bass.Bass("TRN2", target_bir_lowering=False, debug=False)

    def din(name, shape, dt=F32):
        return nc.dram_tensor(name, shape, dt, kind="ExternalInput").ap()

    x0d = din("x0", [S, D])
    wqd = din("wq", [D, D]); wkd = din("wk", [D, D])
    wvd = din("wv", [D, D]); wod = din("wo", [D, D])
    bqt = din("bqt", [NH, DK]); bkt = din("bkt", [NH, DK]); bot = din("bot", [NH, DK])
    bv_row = din("bv_row", [1, D])
    lngt = din("lngt", [NH, DK]); lnbt = din("lnbt", [NH, DK])
    gpd = din("gp", [MLP, 1024], I32); upd = din("up", [MLP, 1024], I32)
    dpd = din("dp", [D, 8192], I32)
    gsd = din("gs", [MLP, 64]); usd = din("us", [MLP, 64]); dsd = din("ds", [D, 512])
    qpos_t = din("qpos_t", [NT, 128])
    cosk = din("cosk", [128, S]); sink = din("sink", [128, S])
    qcosd = din("qcosd", [128, OWN]); qsind = din("qsind", [128, OWN])
    kidxd = din("kidxd", [128, S])
    ident_bd = din("ident_b", [128, 128], BF16)
    ident_fd = din("ident_f", [128, 128])
    ident_hd = din("ident_h", [128, 128])
    rpermTd = din("rpermT", [128, 128], BF16)
    ones_fd = din("ones_f", [1, 128])
    ones_bd = din("ones_b", [1, 128], BF16)
    qseld = din("qsel", [128, 4])
    epsd = din("epscol", [128, 1])

    outd = nc.dram_tensor("out", [OWN, D], F32, kind="ExternalOutput").ap()

    def dint(name, shape, dt):
        return nc.dram_tensor(name, shape, dt, kind="Internal").ap()

    wgT = dint("wgT", [128, 16 * MLP], BF16)
    wuT = dint("wuT", [128, 16 * MLP], BF16)
    wdT = dint("wdT", [MLP, D], BF16)
    wqT = dint("wqT", [128, 16 * D], BF16); wkT = dint("wkT", [128, 16 * D], BF16)
    wvT = dint("wvT", [128, 16 * D], BF16); woT = dint("woT", [128, 16 * D], BF16)
    x1s = dint("x1s", [S, D], F32)
    x1own = dint("x1own", [OWN, D], F32)
    x2s = dint("x2s", [OWN, D], F32)
    kall = dint("kall", [D, S], BF16)
    vall = dint("vall", [S, D], BF16)
    qall = dint("qall", [D, OWN], BF16)
    oall = dint("oall", [D, OWN], BF16)

    from contextlib import ExitStack
    stack = ExitStack()
    ec = stack.enter_context

    ARENA = 206 * KB
    arena_t = ec(nc.sbuf_tensor("arena", [128, ARENA], mybir.dt.uint8))
    arena = arena_t[:]

    def vfw(o, cols):
        return arena[:, o:o + 4 * cols].bitcast(F32)

    def vbw(o, cols):
        return arena[:, o:o + 2 * cols].bitcast(BF16)

    def viw(o, cols):
        return arena[:, o:o + 4 * cols].bitcast(I32)

    off = [0]

    def take(n, align=64):
        o = (off[0] + align - 1) // align * align
        off[0] = o + n
        assert off[0] <= ARENA, f"arena overflow {off[0]}"
        return o

    O_IDB = take(256); O_IDF = take(512); O_IDH = take(512); O_RPT = take(256)
    O_ONES = take(512)
    O_SMALL = take(512)
    O_BQ = take(64); O_BK = take(64); O_BO = take(64); O_QPT = take(64)
    O_LNGC = take(64); O_LNBC = take(64); O_ONEB = take(256)
    O_DN = take(4 * 512)
    O_T1 = take(4 * D)
    O_T2 = take(4 * D)
    O_T3 = take(4 * D)
    O_T4 = take(4 * D)
    O_T5 = take(2 * D)
    O_WBIG = take(64 * KB)
    O_FM = take(2 * 16 * 512)
    O_ACC = take(4 * 16 * 512)
    O_INT = take(8 * 2 * 512)
    O_GB = take(6 * 4 * 512)
    O_EXTRA = take(32 * KB)
    print(f"[kernel] arena used: {off[0]} bytes/partition")

    sb_idb = vbw(O_IDB, 128); sb_idf = vfw(O_IDF, 128); sb_idh = vfw(O_IDH, 128)
    sb_rpt = vbw(O_RPT, 128)
    sb_ones = vfw(O_ONES, 128)
    sb_ramp = vfw(O_SMALL, 1); sb_sign = vfw(O_SMALL + 4, 1)
    sb_eps = vfw(O_SMALL + 44, 1)
    sb_theta = vfw(O_SMALL + 8, 1)
    sb_mu = vfw(O_SMALL + 12, 1); sb_var = vfw(O_SMALL + 16, 1)
    sb_sd = vfw(O_SMALL + 20, 1); sb_nm = vfw(O_SMALL + 24, 1)
    sb_rm = vfw(O_SMALL + 28, 1); sb_l = vfw(O_SMALL + 32, 1)
    sb_rl = vfw(O_SMALL + 36, 1); sb_t0 = vfw(O_SMALL + 40, 1)
    sb_qsel = vfw(O_SMALL + 48, 4)
    sb_bq = vfw(O_BQ, NH); sb_bk = vfw(O_BK, NH); sb_bo = vfw(O_BO, NH)
    sb_qpt = vfw(O_QPT, NT)
    sb_lngc = vfw(O_LNGC, NH); sb_lnbc = vfw(O_LNBC, NH)
    sb_oneb = vbw(O_ONEB, 128)
    v_dn = vfw(O_DN, 512)
    v_xc = vfw(O_T1, D); v_sq = vfw(O_T2, D)
    v_xt = vfw(O_T3, D); v_ut = vfw(O_T4, D); v_ztm = vbw(O_T5, D)
    v_fm = vbw(O_FM, 16 * 512)
    v_acc = vfw(O_ACC, 16 * 512)

    v_kv1 = vfw(O_EXTRA + 16 * KB, 512)
    v_kv2 = vfw(O_EXTRA + 18 * KB, 512)
    v_kv3 = vbw(O_EXTRA + 20 * KB, 512)
    v_kv4 = vbw(O_EXTRA + 21 * KB, 512)
    v_cosb = vfw(O_EXTRA + 22 * KB, 512)
    v_sinb = vfw(O_EXTRA + 24 * KB, 512)
    v_bvb = vbw(O_EXTRA + 26 * KB, D)

    def gbuf(i):
        return vfw(O_GB + i * 2 * KB, 512)

    def intbuf(i):
        return vbw(O_INT + i * 2 * 512, 512)

    v_row = vfw(O_EXTRA, S)
    v_prw = vbw(O_ACC, S)
    v_spt = vbw(O_ACC + 16 * KB, S)
    v_ki = v_dn
    v_msk_off = O_GB
    v_rlb = vfw(O_FM, 128)
    v_rlr = vfw(O_FM + 2 * KB, 128)
    v_oo = vbw(O_FM + 4 * KB, 128)
    v_bl0 = vfw(O_EXTRA, D)
    v_bl1 = vfw(O_EXTRA + 8 * KB, D)
    r_bv = vfw(O_EXTRA, D)
    r_qp = vfw(O_EXTRA + 8 * KB, OWN)
    r_ki = vfw(O_FM, S)

    DQ = O_WBIG
    dq_pk = viw(DQ, 1024)
    dq_sc = vfw(DQ + 4 * KB, 64)
    dq_sce = vfw(DQ + 5 * KB, 1024)
    dq_c = viw(DQ + 9 * KB, 1024)
    dq_t = viw(DQ + 13 * KB, 1024)
    dq_dd = vfw(DQ + 17 * KB, 1024)
    dq_ff = vfw(DQ + 21 * KB, 1024)
    DQ_PL = DQ + 25 * KB

    def dq_pl(i, s=0):
        if s == 0:
            return vbw(DQ_PL + i * 2 * KB, 1024)
        if i < 8:
            return vbw(O_FM + i * 2 * KB, 1024)
        return vbw(O_EXTRA + 8 * KB + (i - 8) * 2 * KB, 1024)

    dq_wt = vbw(DQ + 57 * KB, 1024)
    aw_row = vfw(DQ, D)
    aw_bf = vbw(DQ + 8 * KB, D)
    aw_wt = vbw(DQ + 12 * KB, 512)
    tb_pos = vfw(DQ + 16 * KB, 512)
    tb_fr = vfw(DQ + 18 * KB, 512)
    tb_n = vfw(DQ + 20 * KB, 512)
    tb_red = vfw(DQ + 22 * KB, 512)
    tb_tb = vfw(DQ + 24 * KB, 512)

    O_WG = [O_WBIG, O_WBIG + 8 * KB]
    O_WU = [O_WBIG + 16 * KB, O_WBIG + 24 * KB]
    O_WD = [O_WBIG + 32 * KB + i * 4 * KB for i in range(8)]
    O_WK = O_WBIG
    O_WV = O_WBIG + 16 * KB
    O_WQ = [O_WBIG, O_WBIG + 32 * KB]
    O_K2 = O_WBIG
    O_V2 = O_WBIG + 16 * KB
    O_Q2 = O_WBIG + 32 * KB
    O_WO = O_WBIG
    O_OSB = O_INT

    ps = [ec(nc.psum_tensor(f"ps{i}", [128, 512], F32)) for i in range(8)]

    def psb(i, cols=512):
        return ps[i][:].bitcast(BF16)[:, 0:cols]

    sems = {k: ec(nc.semaphore(k)) for k in ("L", "T", "P", "V", "A")}
    blk = ec(nc.Block())
    b = Builder(sems)

    def flow():
        b.dma(sb_idb, ident_bd)
        b.dma(sb_idf, ident_fd)
        b.dma(sb_idh, ident_hd)
        b.dma(sb_rpt, rpermTd)
        b.dma(sb_ones[0:1, :], ones_fd)
        b.dma(sb_qsel, qseld)
        b.dma(sb_eps, epsd)
        b.dma(sb_oneb[0:1, :], ones_bd)
        for h in range(NH):
            b.dma(sb_bq[:, h:h + 1], bqt[h:h + 1, :].rearrange("o p -> p o"))
            b.dma(sb_bk[:, h:h + 1], bkt[h:h + 1, :].rearrange("o p -> p o"))
            b.dma(sb_bo[:, h:h + 1], bot[h:h + 1, :].rearrange("o p -> p o"))
            b.dma(sb_lngc[:, h:h + 1], lngt[h:h + 1, :].rearrange("o p -> p o"))
            b.dma(sb_lnbc[:, h:h + 1], lnbt[h:h + 1, :].rearrange("o p -> p o"))
        for i in range(NT):
            b.dma(sb_qpt[:, i:i + 1], qpos_t[i:i + 1, :].rearrange("o p -> p o"))
        Lc = b.dma(r_bv[0:1, :], bv_row)

        b.wait("vector", "L", Lc)
        b.ve("tensor_copy", v_bvb[0:1, :], r_bv[0:1, :])
        b.vdone()
        b.wait("tensor", "L", Lc)

        for (wsrc, wdst) in ((wqd, wqT), (wkd, wkT), (wvd, wvT), (wod, woT)):
            for mc in range(16):
                b.wait("sync", "V", b.cnt["V"])
                lm = b.dma(aw_row, wsrc[mc * 128:(mc + 1) * 128, :])
                b.wait("vector", "L", lm)
                b.wait("vector", "T", b.cnt["T"])
                b.ve("tensor_copy", aw_bf, aw_row)
                vm = b.vdone()
                b.wait("tensor", "V", vm)
                for g in range(4):
                    pm = None
                    for q in range(4):
                        kc = g * 4 + q
                        pm = b.tr(psb(2)[:, q * 128:(q + 1) * 128],
                                  aw_bf[:, kc * 128:(kc + 1) * 128], sb_idb,
                                  inc=(q == 3))
                    b.wait("scalar", "P", pm)
                    b.wait("scalar", "T", b.cnt["T"])
                    b.ac("copy", aw_wt, psb(2))
                    am = b.adone()
                    b.wait("tensor", "A", am)
                    b.wait("sync", "A", am)
                    wv4 = wdst.rearrange("p (mg kc m) -> p mg kc m", kc=16, m=512)
                    for q in range(4):
                        kc = g * 4 + q
                        b.dma(wv4[:, mc // 4, kc,
                                  (mc % 4) * 128:(mc % 4) * 128 + 128],
                              aw_wt[:, q * 128:(q + 1) * 128], sem="T")

        state = {"pl_used": [0, 0], "si": 0}

        def dequant(packed, scales, wdstT, NROW, JW, gmajor=False):
            nchunks = NROW // 128
            nslices = JW // 1024
            for g in range(nchunks // 8):
                for js in range(nslices):
                    si = state["si"] % 2
                    vplanes = None
                    for s8 in range(8):
                        n0 = (g * 8 + s8) * 128
                        b.wait("sync", "V", b.cnt["V"])
                        lm = b.dma(dq_pk, packed[n0:n0 + 128, js * 1024:(js + 1) * 1024])
                        ls = b.dma(dq_sc, scales[n0:n0 + 128, js * 64:(js + 1) * 64])
                        b.wait("vector", "L", ls)
                        if state["pl_used"][si]:
                            b.wait("vector", "P", state["pl_used"][si])
                        b.ve("tensor_copy",
                             dq_sce.rearrange("p (bb r) -> p bb r", r=16),
                             dq_sc.unsqueeze(2).broadcast_to([128, 64, 16]))
                        for nib in range(2):
                            if nib == 0:
                                b.ve("tensor_scalar", dq_c, dq_pk, 15, None,
                                     ALU.bitwise_and)
                            else:
                                b.ve("tensor_scalar", dq_c, dq_pk, 4, None,
                                     ALU.logical_shift_right)
                            b.ve("tensor_scalar", dq_t, dq_c, 7, 22,
                                 ALU.bitwise_and, ALU.logical_shift_left)
                            b.ve("tensor_scalar", dq_t, dq_t, 0x3F000000, None, ALU.add)
                            u = dq_t.bitcast(F32)
                            b.ve("tensor_scalar", dq_dd, u, 1.0, 0.0,
                                 ALU.subtract, ALU.min)
                            b.ve("tensor_tensor", dq_ff, u, dq_dd, ALU.add)
                            b.ve("tensor_scalar", dq_c, dq_c, 8, 28,
                                 ALU.bitwise_and, ALU.logical_shift_left)
                            b.ve("tensor_tensor", dq_ff.bitcast(I32),
                                 dq_ff.bitcast(I32), dq_c, ALU.bitwise_or)
                            b.ve("tensor_tensor", dq_pl(s8 * 2 + nib, si), dq_ff,
                                 dq_sce, ALU.mult)
                        vplanes = b.vdone()
                    pm = None
                    for nib in range(2):
                        for jc in range(8):
                            b.wait("tensor", "V", vplanes)
                            for s8 in range(8):
                                pm = b.tr(psb(3, 1024)[:, s8 * 128:(s8 + 1) * 128],
                                          dq_pl(s8 * 2 + nib, si)[:, jc * 128:(jc + 1) * 128],
                                          sb_idb, inc=(s8 == 7))
                            b.wait("scalar", "P", pm)
                            b.wait("scalar", "T", b.cnt["T"])
                            b.ac("copy", dq_wt, psb(3, 1024))
                            am = b.adone()
                            b.wait("tensor", "A", am)
                            b.wait("sync", "A", am)
                            if gmajor:
                                gv = wdstT.rearrange(
                                    "(r64 two) (gg kc n) -> two r64 gg kc n",
                                    two=2, kc=16, n=256)
                                for half in range(2):
                                    b.dma(gv[nib, :, g * 4:(g + 1) * 4,
                                             2 * jc + half, :],
                                          dq_wt[half * 64:(half + 1) * 64, :]
                                          .rearrange("p (gg n) -> p gg n", n=256),
                                          sem="T")
                            else:
                                jrow = js * 1024 + jc * 128
                                b.dma(wdstT.rearrange("(k2 two) n -> two k2 n", two=2)
                                      [nib, jrow:jrow + 128, g * 1024:(g + 1) * 1024],
                                      dq_wt, sem="T")
                    state["pl_used"][si] = pm
                    state["si"] += 1

        dequant(gpd, gsd, wgT, MLP, 1024, gmajor=True)
        dequant(upd, usd, wuT, MLP, 1024, gmajor=True)
        dequant(dpd, dsd, wdT, D, 8192)
        Tw = b.cnt["T"]

        def layernorm(xt, outd_ap, gamma=None, beta=None):
            b.ve("reduce_sum", sb_mu, xt, AX.X)
            b.vdrain()
            b.ve("tensor_scalar", sb_mu, sb_mu, 1.0 / D, None, ALU.mult)
            b.vdrain()
            b.ve("tensor_scalar", v_xc, xt, sb_mu[:, 0:1], None, ALU.subtract)
            vm = b.vdone()
            b.wait("scalar", "V", vm)
            b.ac("activation", v_sq, v_xc, AF.Square, bias=0.0, scale=1.0)
            am = b.adone()
            b.wait("vector", "A", am)
            b.ve("reduce_sum", sb_var, v_sq, AX.X)
            vm = b.vdone()
            b.wait("scalar", "V", vm)
            b.ac("activation", sb_sd, sb_var, AF.Sqrt, bias=sb_eps[:, 0:1], scale=1.0 / D)
            am = b.adone()
            b.wait("vector", "A", am)
            b.ve("reciprocal", sb_sd, sb_sd)
            b.vdrain()
            if gamma is None:
                b.ve("tensor_scalar", outd_ap, v_xc, sb_sd[:, 0:1], None, ALU.mult)
            else:
                b.ve("tensor_scalar", v_xc, v_xc, sb_sd[:, 0:1], None, ALU.mult)
                b.ve("tensor_tensor", v_sq, v_xc, gamma, ALU.mult)
                b.ve("tensor_tensor", outd_ap, v_sq, beta, ALU.add)
            return b.vdone()

        def tm_to_fm(src_bf, sub, vready, gb=False, fm_cols=512):
            b.wait("tensor", "V", vready)
            am = None
            for g in range(4):
                pm = None
                for q in range(4):
                    kc = g * 4 + q
                    pm = b.tr(psb(6)[:, q * 128:(q + 1) * 128],
                              src_bf[:, kc * 128:(kc + 1) * 128], sb_idb,
                              inc=(q == 3))
                b.wait("scalar", "P", pm)
                if gb:
                    for q in range(4):
                        kc = g * 4 + q
                        b.ac("activation",
                             v_fm[:, kc * fm_cols + sub * 128:
                                  kc * fm_cols + sub * 128 + 128],
                             psb(6)[:, q * 128:(q + 1) * 128], AF.Identity,
                             bias=sb_lnbc[:, kc:kc + 1], scale=sb_lngc[:, kc:kc + 1])
                else:
                    b.ac("copy",
                         v_fm.rearrange("p (kc t) -> p kc t", t=fm_cols)
                         [:, g * 4:(g + 1) * 4, sub * 128:sub * 128 + 128],
                         psb(6).rearrange("p (q t) -> p q t", t=128))
                am = b.adone()
                b.wait("tensor", "A", am)
            return am

        def emit_ffn(fm_ready_a, guard_P=0):
            NWG = 2
            NGW = 128 // NWG
            lw_wg = [0] * (NGW + 2)
            lw_wd = [0] * 129
            pm_grp = [0] * (NGW + 1)
            vm_inter = [0] * 130
            am_down = [0] * 17
            pm_down = [0] * 17

            def wgv(g):
                return vbw(O_WG[g % 2], 16 * 256)

            def wuv(g):
                return vbw(O_WU[g % 2], 16 * 256)

            def wdv(c):
                return vbw(O_WD[c % 8], 2048)

            def load_wgu(g):
                b.dma(wgv(g), wgT[:, g * 4096:(g + 1) * 4096])
                return b.dma(wuv(g), wuT[:, g * 4096:(g + 1) * 4096])

            b.wait("sync", "P", guard_P)
            lw_wg[0] = load_wgu(0)
            lw_wg[1] = load_wgu(1)
            for c in range(8):
                lw_wd[c] = b.dma(wdv(c), wdT[c * 128:(c + 1) * 128, :])

            b.wait("tensor", "A", fm_ready_a)
            for g in range(NGW):
                b.wait("tensor", "L", lw_wg[g])
                for q in range(NWG):
                    c = g * NWG + q
                    rot = c % 2
                    pgp = ps[rot]
                    pup = ps[2 + rot]
                    if c >= 2:
                        b.wait("tensor", "V", vm_inter[c - 2])
                    pmg = None
                    for kc in range(16):
                        pmg = b.mm(pgp[:],
                                   wgv(g)[:, kc * 256 + q * 128: kc * 256 + q * 128 + 128],
                                   v_fm[:, kc * 512:(kc + 1) * 512],
                                   start=(kc == 0), stop=(kc == 15), inc=(kc == 15))
                    pmu = None
                    for kc in range(16):
                        pmu = b.mm(pup[:],
                                   wuv(g)[:, kc * 256 + q * 128: kc * 256 + q * 128 + 128],
                                   v_fm[:, kc * 512:(kc + 1) * 512],
                                   start=(kc == 0), stop=(kc == 15), inc=(kc == 15))
                    gs_ = gbuf(0); gw_ = gbuf(1); gi_ = gbuf(2)
                    gt_ = gbuf(3); gu_ = gbuf(4); ga_ = gbuf(5)
                    b.wait("scalar", "P", pmg)
                    if c >= 1:
                        b.wait("scalar", "V", vm_inter[c - 1])
                    b.ac("activation", gs_, pgp[:], AF.Square, bias=0.0, scale=1.0)
                    b.wait("scalar", "P", pmu)
                    b.ac("activation", gu_, pup[:], AF.Copy, bias=0.0, scale=0.5)
                    am1 = b.adone()
                    b.wait("vector", "A", am1)
                    b.ve("tensor_scalar", gw_, gs_, G_C1, G_C0, ALU.mult, ALU.add)
                    b.ve("tensor_tensor", gi_, gw_, pgp[:], ALU.mult)
                    vm1 = b.vdone()
                    b.wait("scalar", "V", vm1)
                    b.ac("activation", gt_, gi_, AF.Tanh, bias=0.0, scale=1.0)
                    am2 = b.adone()
                    b.wait("vector", "A", am2)
                    b.ve("scalar_tensor_tensor", ga_, gt_, 1.0, pgp[:],
                         ALU.add, ALU.mult)
                    b.ve("tensor_tensor", intbuf(c % 8), ga_, gu_, ALU.mult)
                    vm_inter[c] = b.vdone()
                pm_grp[g] = b.cnt["P"]
                if g + 2 < NGW:
                    b.wait("sync", "P", pm_grp[g])
                    lw_wg[g + 2] = load_wgu(g + 2)
                if (g + 1) * NWG % 8 == 0:
                    grp = ((g + 1) * NWG) // 8 - 1
                    c_hi = grp * 8 + 7
                    b.wait("tensor", "V", vm_inter[c_hi])
                    b.wait("tensor", "L", lw_wd[c_hi])
                    if grp >= 2:
                        b.wait("tensor", "A", am_down[grp - 2])
                    pd = ps[4 + grp % 2]
                    for j in range(16):
                        pm = None
                        for cc in range(8):
                            c2 = grp * 8 + cc
                            pm = b.mm(pd[:], wdv(c2)[:, j * 128:(j + 1) * 128],
                                      intbuf(c2 % 8),
                                      start=(cc == 0), stop=(cc == 7), inc=(cc == 7))
                        b.wait("scalar", "P", pm)
                        if grp > 0 or j > 1:
                            b.wait("scalar", "V", b.cnt["V"])
                        b.ac("copy", v_dn, pd[:])
                        am = b.adone()
                        b.wait("vector", "A", am)
                        if grp == 0:
                            b.ve("tensor_copy", v_acc[:, j * 512:(j + 1) * 512], v_dn)
                        else:
                            b.ve("tensor_tensor", v_acc[:, j * 512:(j + 1) * 512],
                                 v_acc[:, j * 512:(j + 1) * 512], v_dn, ALU.add)
                        vmj = b.vdone()
                        b.wait("tensor", "A", am)
                    am_down[grp] = b.cnt["A"]
                    pm_down[grp] = b.cnt["P"]
                    if c_hi + 1 < 128:
                        b.wait("sync", "P", pm_down[grp])
                        for cpre in range(c_hi + 1, min(c_hi + 9, 128)):
                            lw_wd[cpre] = b.dma(wdv(cpre),
                                                wdT[cpre * 128:(cpre + 1) * 128, :])
            return b.cnt["P"], b.cnt["V"]

        b.wait("sync", "T", Tw)
        vk_used = 0
        vv_used = 0
        for bb in range(NBLK):
            b.wait("sync", "V", vk_used)
            b.dma(v_cosb, cosk[:, bb * 512:(bb + 1) * 512])
            lcs = b.dma(v_sinb, sink[:, bb * 512:(bb + 1) * 512])
            for sub in range(4):
                b.wait("sync", "V", b.cnt["V"])
                lm = b.dma(v_xt, x0d[bb * 512 + sub * 128: bb * 512 + sub * 128 + 128, :])
                b.wait("vector", "L", lm)
                b.wait("vector", "A", b.cnt["A"])
                vml = layernorm(v_xt, v_ztm)
                fm_a = tm_to_fm(v_ztm, sub, vml)
            pffn, vffn = emit_ffn(fm_a, b.cnt["P"])
            b.wait("tensor", "V", vffn)
            b.wait("tensor", "A", b.cnt["A"])
            for sub in range(4):
                b.wait("sync", "V", b.cnt["V"])
                b.wait("sync", "T", b.cnt["T"])
                lm = b.dma(v_xt, x0d[bb * 512 + sub * 128: bb * 512 + sub * 128 + 128, :])
                for jg in range(4):
                    pm = None
                    for q in range(4):
                        j = jg * 4 + q
                        pm = b.tr(ps[7][:, q * 128:(q + 1) * 128],
                                  v_acc.rearrange("p (j t) -> p j t", t=512)
                                  [:, j, sub * 128:sub * 128 + 128],
                                  sb_idh, inc=(q == 3))
                    b.wait("vector", "P", pm)
                    b.wait("vector", "L", lm)
                    b.ve("scalar_tensor_tensor", v_xt[:, jg * 512:(jg + 1) * 512],
                         ps[7][:], 0.5, v_xt[:, jg * 512:(jg + 1) * 512],
                         ALU.mult, ALU.add)
                    vm = b.vdone()
                    b.wait("tensor", "V", vm)
                b.wait("sync", "V", b.cnt["V"])
                b.dma(x1s[bb * 512 + sub * 128: bb * 512 + sub * 128 + 128, :],
                      v_xt, sem="T")
                layernorm(v_xt, v_ut)
                vz = layernorm(v_ut, v_ztm)
                fm_a = tm_to_fm(v_ztm, sub, vz, gb=True)
            sWK = vbw(O_WK, 16 * 512)
            sWV = vbw(O_WV, 16 * 512)
            for hg in range(4):
                b.wait("sync", "P", b.cnt["P"])
                lwk = b.dma(sWK, wkT[:, hg * 8192:(hg + 1) * 8192])
                for hh in range(4):
                    h = hg * 4 + hh
                    b.wait("tensor", "L", lwk)
                    b.wait("tensor", "A", fm_a)
                    if vk_used:
                        b.wait("tensor", "V", vk_used)
                    pm = None
                    for kc in range(16):
                        pm = b.mm(ps[7][:],
                                  sWK[:, kc * 512 + hh * 128: kc * 512 + hh * 128 + 128],
                                  v_fm[:, kc * 512:(kc + 1) * 512],
                                  start=(kc == 0), stop=(kc == 15), inc=(kc == 15))
                    b.wait("scalar", "P", pm)
                    b.wait("scalar", "V", vk_used if vk_used else vffn)
                    b.ac("activation", v_kv1, ps[7][:], AF.Identity,
                         bias=sb_bk[:, h:h + 1], scale=1.0)
                    b.ac("copy", v_kv3, v_kv1)
                    am = b.adone()
                    b.wait("tensor", "A", am)
                    pmr = b.mm(ps[4][:], sb_rpt, v_kv3, start=True, stop=True, inc=True)
                    b.wait("vector", "A", am)
                    b.wait("vector", "P", pmr)
                    b.wait("vector", "L", lcs)
                    b.wait("vector", "T", b.cnt["T"])
                    b.ve("tensor_tensor", v_kv2, v_kv1, v_cosb, ALU.mult)
                    b.ve("tensor_tensor", v_kv1, ps[4][:], v_sinb, ALU.mult)
                    b.ve("tensor_tensor", v_kv4, v_kv2, v_kv1, ALU.add)
                    vk_used = b.vdone()
                    b.wait("sync", "V", vk_used)
                    b.dma(kall[h * 128:(h + 1) * 128, bb * 512:(bb + 1) * 512],
                          v_kv4, sem="T")
            for dv in range(4):
                b.wait("sync", "P", b.cnt["P"])
                lwv = b.dma(sWV, wvT[:, dv * 8192:(dv + 1) * 8192])
                for sub in range(4):
                    b.wait("tensor", "L", lwv)
                    if vv_used:
                        b.wait("tensor", "V", vv_used)
                    pm = None
                    for kc in range(16):
                        b.mm(ps[5][:],
                             v_fm[:, kc * 512 + sub * 128: kc * 512 + sub * 128 + 128],
                             sWV[:, kc * 512:(kc + 1) * 512],
                             start=(kc == 0), stop=False)
                    pm = b.mm(ps[5][:], sb_oneb[0:1, :],
                              v_bvb[0:1, dv * 512:(dv + 1) * 512],
                              start=False, stop=True, inc=True)
                    b.wait("vector", "P", pm)
                    b.wait("vector", "T", b.cnt["T"])
                    b.ve("tensor_copy", v_kv4, ps[5][:])
                    vv_used = b.vdone()
                    b.wait("sync", "V", vv_used)
                    b.dma(vall[bb * 512 + sub * 128: bb * 512 + sub * 128 + 128,
                               dv * 512:(dv + 1) * 512], v_kv4, sem="T")
        T_p1 = b.cnt["T"]

        b.wait("sync", "T", T_p1)
        sWQ0 = vbw(O_WQ[0], 16 * 1024)
        sWQ1 = vbw(O_WQ[1], 16 * 1024)
        vq_used = 0
        for i in range(NT):
            b.wait("vector", "T", b.cnt["T"])
            for j in range(4):
                bl = v_bl0 if j % 2 == 0 else v_bl1
                b.wait("sync", "V", b.cnt["V"] if j >= 2 else 0)
                lm = b.dma(bl, x1s[i * 512 + j * 128: i * 512 + j * 128 + 128, :])
                b.wait("vector", "L", lm)
                if j == 0:
                    b.ve("tensor_scalar", v_xt, bl, sb_qsel[:, 0:1], None, ALU.mult)
                else:
                    b.ve("tensor_scalar", v_ut, bl, sb_qsel[:, j:j + 1], None, ALU.mult)
                    b.ve("tensor_tensor", v_xt, v_xt, v_ut, ALU.add)
                b.vdone()
            vm = b.cnt["V"]
            b.wait("sync", "V", vm)
            b.dma(x1own[i * 128:(i + 1) * 128, :], v_xt, sem="T")
            layernorm(v_xt, v_ut)
            vz2 = layernorm(v_ut, v_ztm)
            zf = v_fm
            b.wait("tensor", "V", vz2)
            zready = None
            for g in range(4):
                pm = None
                for q in range(4):
                    kc = g * 4 + q
                    pm = b.tr(psb(6)[:, q * 128:(q + 1) * 128],
                              v_ztm[:, kc * 128:(kc + 1) * 128], sb_idb, inc=(q == 3))
                b.wait("scalar", "P", pm)
                for q in range(4):
                    kc = g * 4 + q
                    b.ac("activation", zf[:, kc * 128:(kc + 1) * 128],
                         psb(6)[:, q * 128:(q + 1) * 128], AF.Identity,
                         bias=sb_lnbc[:, kc:kc + 1], scale=sb_lngc[:, kc:kc + 1])
                zready = b.adone()
                b.wait("tensor", "A", zready)
            b.wait("sync", "V", vq_used)
            b.wait("sync", "P", b.cnt["P"])
            lq = b.dma(v_cosb[:, 0:128], qcosd[:, i * 128:(i + 1) * 128])
            lq = b.dma(v_sinb[:, 0:128], qsind[:, i * 128:(i + 1) * 128])
            lwq = b.dma(sWQ0, wqT[:, 0:16384])
            lwq1 = b.dma(sWQ1, wqT[:, 16384:32768])
            for h in range(NH):
                sw = sWQ0 if h < 8 else sWQ1
                hh = h % 8
                ho = (hh // 4) * 8192 + (hh % 4) * 128
                b.wait("tensor", "L", lwq if h < 8 else lwq1)
                b.wait("tensor", "A", zready)
                if vq_used:
                    b.wait("tensor", "V", vq_used)
                pm = None
                for kc in range(16):
                    pm = b.mm(ps[7][:, 0:128],
                              sw[:, ho + kc * 512: ho + kc * 512 + 128],
                              zf[:, kc * 128:(kc + 1) * 128],
                              start=(kc == 0), stop=(kc == 15), inc=(kc == 15))
                b.wait("scalar", "P", pm)
                if vq_used:
                    b.wait("scalar", "V", vq_used)
                b.ac("activation", v_kv1[:, 0:128], ps[7][:, 0:128], AF.Identity,
                     bias=sb_bq[:, h:h + 1], scale=1.0)
                b.ac("copy", v_kv3[:, 0:128], v_kv1[:, 0:128])
                am = b.adone()
                b.wait("tensor", "A", am)
                pmr = b.mm(ps[4][:, 0:128], sb_rpt, v_kv3[:, 0:128],
                           start=True, stop=True, inc=True)
                b.wait("vector", "A", am)
                b.wait("vector", "P", pmr)
                b.wait("vector", "L", lq)
                b.wait("vector", "T", b.cnt["T"])
                b.ve("tensor_tensor", v_kv2[:, 0:128], v_kv1[:, 0:128],
                     v_cosb[:, 0:128], ALU.mult)
                b.ve("tensor_tensor", v_kv1[:, 0:128], ps[4][:, 0:128],
                     v_sinb[:, 0:128], ALU.mult)
                b.ve("tensor_tensor", v_kv4[:, 0:128], v_kv2[:, 0:128],
                     v_kv1[:, 0:128], ALU.add)
                vq_used = b.vdone()
                b.wait("sync", "V", vq_used)
                b.dma(qall[h * 128:(h + 1) * 128, i * 128:(i + 1) * 128],
                      v_kv4[:, 0:128], sem="T")
        T_p2a = b.cnt["T"]

        sK = vbw(O_K2, S)
        sV = vbw(O_V2, S)
        sQ = vbw(O_Q2, OWN)
        b.wait("sync", "T", T_p2a)
        for h in range(NH):
            b.wait("sync", "P", b.cnt["P"])
            lK = b.dma(sK, kall[h * 128:(h + 1) * 128, :])
            lV = b.dma(sV.rearrange("p (jt dd) -> p jt dd", dd=128),
                       vall.rearrange("(jt p) dd -> p jt dd", p=128)
                       [:, :, h * 128:(h + 1) * 128])
            lQ = b.dma(sQ, qall[h * 128:(h + 1) * 128, :])
            for i in range(NT):
                nch = i + 1
                b.wait("tensor", "L", lQ)
                b.wait("tensor", "L", lK)
                b.wait("tensor", "A", b.cnt["A"])
                am_evac = [0] * nch
                for c in range(nch):
                    pss = ps[c % 2]
                    if c >= 2:
                        b.wait("tensor", "A", am_evac[c - 2])
                    pm = b.mm(pss[:], sQ[:, i * 128:(i + 1) * 128],
                              sK[:, c * 512:(c + 1) * 512],
                              start=True, stop=True, inc=True)
                    b.wait("scalar", "P", pm)
                    b.ac("copy", v_row[:, c * 512:(c + 1) * 512], pss[:])
                    am_evac[c] = b.adone()
                lk = b.dma(v_ki, kidxd[:, i * 512:(i + 1) * 512])
                b.wait("vector", "L", lk)
                b.wait("vector", "A", am_evac[nch - 1])
                v_msk = vfw(O_GB, 512)
                b.ve("tensor_scalar", v_msk, v_ki, sb_qpt[:, i:i + 1], None, ALU.is_gt)
                b.ve("scalar_tensor_tensor", v_row[:, i * 512:(i + 1) * 512],
                     v_msk, NEGBIG, v_row[:, i * 512:(i + 1) * 512],
                     ALU.mult, ALU.add)
                b.ve("reduce_max", sb_rm, v_row[:, 0:nch * 512], AX.X)
                b.vdrain()
                b.ve("tensor_scalar", sb_nm, sb_rm, -1.0, None, ALU.mult)
                vm = b.vdone()
                b.wait("scalar", "V", vm)
                b.ac("activation", v_prw[:, 0:nch * 512], v_row[:, 0:nch * 512],
                     AF.Exp, bias=sb_nm[:, 0:1], scale=1.0, accum_out=sb_l)
                am_exp = b.adone()
                b.wait("vector", "A", am_exp)
                b.ve("reciprocal", sb_rl, sb_l)
                vrl = b.vdone()
                b.wait("tensor", "V", vrl)
                pm = b.tr(ps[4][0:1, 0:128], sb_rl, sb_idf, inc=True)
                b.wait("scalar", "P", pm)
                b.ac("copy", v_rlr[0:1, :], ps[4][0:1, 0:128])
                am = b.adone()
                b.wait("tensor", "A", am)
                pm = b.mm(ps[5][:, 0:128], sb_ones[0:1, :], v_rlr[0:1, :],
                          start=True, stop=True, inc=True)
                b.wait("scalar", "P", pm)
                b.ac("copy", v_rlb, ps[5][:, 0:128])
                am_rlb = b.adone()
                for c in range(nch):
                    if c > 0:
                        b.wait("tensor", "A", b.cnt["A"])
                    pm = None
                    for half in range(2):
                        for jt in range(2):
                            sl = c * 4 + half * 2 + jt
                            pm = b.tr(psb(2)[:, (half * 2 + jt) * 128:
                                             (half * 2 + jt + 1) * 128],
                                      v_prw[:, sl * 128:(sl + 1) * 128],
                                      sb_idb, inc=(half == 1 and jt == 1))
                    b.wait("scalar", "P", pm)
                    b.ac("copy", v_spt[:, c * 512:(c + 1) * 512], psb(2))
                    b.adone()
                am_pt = b.cnt["A"]
                b.wait("tensor", "A", am_pt)
                b.wait("tensor", "L", lV)
                first = True
                pmO = None
                for c in range(nch):
                    for jt in range(4):
                        sl = c * 4 + jt
                        pmO = b.mm(ps[3][:, 0:128],
                                   sV[:, sl * 128:(sl + 1) * 128],
                                   v_spt[:, sl * 128:(sl + 1) * 128],
                                   start=first, stop=(c == nch - 1 and jt == 3),
                                   inc=(c == nch - 1 and jt == 3))
                        first = False
                b.wait("vector", "P", pmO)
                b.wait("vector", "A", am_rlb)
                b.wait("vector", "T", b.cnt["T"])
                b.ve("tensor_tensor", v_oo, ps[3][:, 0:128], v_rlb, ALU.mult)
                vo = b.vdone()
                b.wait("sync", "V", vo)
                b.dma(oall[h * 128:(h + 1) * 128, i * 128:(i + 1) * 128],
                      v_oo, sem="T")
        T_p2b = b.cnt["T"]

        sWo = vbw(O_WO, 16 * 2048)
        sOsb = vbw(O_OSB, 16 * 512)
        b.wait("sync", "T", T_p2b)
        for gq in range(4):
            b.wait("sync", "P", b.cnt["P"])
            lwo = b.dma(sWo, woT[:, :])
            lo = b.dma(sOsb.rearrange("p (kc t) -> p kc t", t=512),
                       oall.rearrange("(kc p) t -> p kc t", p=128)
                       [:, :, gq * 512:(gq + 1) * 512])
            am_dj = [0] * 16
            for j in range(16):
                pdj = ps[j % 2]
                jo = (j // 4) * 8192 + (j % 4) * 128
                b.wait("tensor", "L", lwo)
                b.wait("tensor", "L", lo)
                if j >= 2:
                    b.wait("tensor", "A", am_dj[j - 2])
                pm = None
                for kc in range(16):
                    pm = b.mm(pdj[:],
                              sWo[:, jo + kc * 512: jo + kc * 512 + 128],
                              sOsb[:, kc * 512:(kc + 1) * 512],
                              start=(kc == 0), stop=(kc == 15), inc=(kc == 15))
                b.wait("scalar", "P", pm)
                b.ac("activation", v_acc[:, j * 512:(j + 1) * 512], pdj[:], AF.Identity,
                     bias=sb_bo[:, j:j + 1], scale=1.0)
                am_dj[j] = b.adone()
            am_delta = b.cnt["A"]
            for sub in range(4):
                b.wait("sync", "V", b.cnt["V"])
                b.wait("sync", "T", b.cnt["T"])
                lm = b.dma(v_xt, x1own[gq * 512 + sub * 128: gq * 512 + sub * 128 + 128, :])
                b.wait("vector", "L", lm)
                layernorm(v_xt, v_ut)
                b.ve("tensor_tensor", v_xt, v_xt, v_ut, ALU.add)
                vxu = b.vdone()
                b.wait("tensor", "A", am_delta)
                for jg in range(4):
                    pm = None
                    for q in range(4):
                        j = jg * 4 + q
                        pm = b.tr(ps[6][:, q * 128:(q + 1) * 128],
                                  v_acc.rearrange("p (j t) -> p j t", t=512)
                                  [:, j, sub * 128:sub * 128 + 128],
                                  sb_idf, inc=(q == 3))
                    b.wait("vector", "P", pm)
                    b.wait("vector", "V", vxu)
                    b.ve("tensor_tensor", v_xt[:, jg * 512:(jg + 1) * 512],
                         v_xt[:, jg * 512:(jg + 1) * 512], ps[6][:], ALU.add)
                    vm = b.vdone()
                    b.wait("tensor", "V", vm)
                b.wait("sync", "V", b.cnt["V"])
                b.dma(x2s[gq * 512 + sub * 128: gq * 512 + sub * 128 + 128, :],
                      v_xt, sem="T")
                vml = layernorm(v_xt, v_ztm)
                fm_a = tm_to_fm(v_ztm, sub, vml)
            pffn, vffn = emit_ffn(fm_a, b.cnt["P"])
            b.wait("tensor", "V", vffn)
            b.wait("tensor", "A", b.cnt["A"])
            Tx2 = b.cnt["T"]
            for sub in range(4):
                b.wait("sync", "T", Tx2)
                b.wait("sync", "V", b.cnt["V"])
                lm = b.dma(v_xt, x2s[gq * 512 + sub * 128: gq * 512 + sub * 128 + 128, :])
                b.wait("vector", "L", lm)
                for jg in range(4):
                    pm = None
                    for q in range(4):
                        j = jg * 4 + q
                        pm = b.tr(ps[6][:, q * 128:(q + 1) * 128],
                                  v_acc.rearrange("p (j t) -> p j t", t=512)
                                  [:, j, sub * 128:sub * 128 + 128],
                                  sb_idh, inc=(q == 3))
                    b.wait("vector", "P", pm)
                    b.ve("scalar_tensor_tensor", v_xt[:, jg * 512:(jg + 1) * 512],
                         ps[6][:], 0.5, v_xt[:, jg * 512:(jg + 1) * 512],
                         ALU.mult, ALU.add)
                    vm = b.vdone()
                    b.wait("tensor", "V", vm)
                b.wait("sync", "V", b.cnt["V"])
                b.dma(outd[gq * 512 + sub * 128: gq * 512 + sub * 128 + 128, :],
                      v_xt, sem="T")
        b.wait("sync", "T", b.cnt["T"])

    b.start(None, None)
    flow()
    final_counts = dict(b.cnt)

    for engname in ENGINES_EMIT:
        def emit(eng, en=engname):
            b.start(en, eng)
            flow()
            assert b.cnt == final_counts, f"nondeterministic flow on {en}: {b.cnt} vs {final_counts}"
        getattr(blk, engname)(emit)

    stack.close()
    return nc


_NC_CACHE = {}


def kernel(**inputs):
    x = np.asarray(inputs["L0_attn_h"])
    wq = np.ascontiguousarray(inputs["L0_attn_wq"], np.float32)
    wk = np.ascontiguousarray(inputs["L0_attn_wk"], np.float32)
    wv = np.ascontiguousarray(inputs["L0_attn_wv"], np.float32)
    wo = np.ascontiguousarray(inputs["L0_attn_wo"], np.float32)
    bq = np.ascontiguousarray(inputs["L0_attn_bq"], np.float32).reshape(NH, DK)
    bk = np.ascontiguousarray(inputs["L0_attn_bk"], np.float32).reshape(NH, DK)
    bv = np.ascontiguousarray(inputs["L0_attn_bv"], np.float32)
    bo = np.ascontiguousarray(inputs["L0_attn_bo"], np.float32).reshape(NH, DK)
    lng = np.ascontiguousarray(inputs["L0_attn_ln_g"], np.float32)
    lnb = np.ascontiguousarray(inputs["L0_attn_ln_b"], np.float32)
    gp = np.ascontiguousarray(inputs["L0_ffn_gate_W_packed"], np.int32)
    gs = np.ascontiguousarray(inputs["L0_ffn_gate_scales"], np.float32)
    up = np.ascontiguousarray(inputs["L0_ffn_up_W_packed"], np.int32)
    us = np.ascontiguousarray(inputs["L0_ffn_up_scales"], np.float32)
    dp = np.ascontiguousarray(inputs["L0_ffn_down_W_packed"], np.int32)
    ds = np.ascontiguousarray(inputs["L0_ffn_down_scales"], np.float32)

    if "nc" not in _NC_CACHE:
        _NC_CACHE["nc"] = build_program()
    nc = _NC_CACHE["nc"]

    ident_b = np.eye(128).astype(ml_dtypes.bfloat16)
    ident_f = np.eye(128, dtype=np.float32)
    ident_h = (0.5 * np.eye(128)).astype(np.float32)
    R = np.zeros((128, 128), np.float32)
    for d in range(128):
        R[d, (d + 64) % 128] = 1.0
    rpermT = np.ascontiguousarray(R.T).astype(ml_dtypes.bfloat16)
    ones_f = np.ones((1, 128), np.float32)

    theta = 1.0 / 10000.0 ** (np.arange(0, 128, 2, dtype=np.float64) / 128.0)
    theta2 = np.concatenate([theta, theta])
    signv = np.where(np.arange(128) < 64, -1.0, 1.0)
    ang_k = theta2[:, None] * np.arange(S, dtype=np.float64)[None, :]
    cosk_h = np.ascontiguousarray(np.cos(ang_k), np.float32)
    sink_h = np.ascontiguousarray(np.sin(ang_k) * signv[:, None], np.float32)
    kidx_h = np.ascontiguousarray(
        np.broadcast_to(np.arange(S, dtype=np.float32)[None, :], (128, S)))

    in_maps = []
    for c in range(8):
        seq = c // 4
        r = c % 4
        own_tiles = [4 * i + r for i in range(NT)]
        qpos_tl = np.stack([
            np.arange(g * 128, (g + 1) * 128, dtype=np.float32) for g in own_tiles])
        qpos = qpos_tl.reshape(-1).astype(np.float64)
        ang_q = theta2[:, None] * qpos[None, :]
        qcos_h = np.ascontiguousarray(np.cos(ang_q) * INVSQ, np.float32)
        qsin_h = np.ascontiguousarray(np.sin(ang_q) * signv[:, None] * INVSQ,
                                      np.float32)
        qsel = np.zeros((128, 4), np.float32)
        qsel[:, r] = 1.0
        in_maps.append(dict(
            x0=np.ascontiguousarray(x[seq], np.float32),
            wq=wq, wk=wk, wv=wv, wo=wo,
            bqt=bq, bkt=bk, bot=bo,
            bv_row=bv.reshape(1, D),
            lngt=lng.reshape(NH, DK), lnbt=lnb.reshape(NH, DK),
            gp=gp, up=up, dp=dp, gs=gs, us=us, ds=ds,
            qpos_t=qpos_tl,
            cosk=cosk_h, sink=sink_h, kidxd=kidx_h,
            qcosd=qcos_h, qsind=qsin_h,
            epscol=np.full((128, 1), EPS, np.float32),
            ident_b=ident_b, ident_f=ident_f, ident_h=ident_h,
            rpermT=rpermT, ones_f=ones_f,
            ones_b=np.ones((1, 128), ml_dtypes.bfloat16), qsel=qsel,
        ))

    res = run_bass_kernel_spmd(nc, in_maps, core_ids=list(range(8)))

    final = np.empty((2, S, D), np.float32)
    for c in range(8):
        seq = c // 4
        r = c % 4
        o = np.asarray(res.results[c]["out"], np.float32)
        for i in range(NT):
            g = 4 * i + r
            final[seq, g * 128:(g + 1) * 128, :] = o[i * 128:(i + 1) * 128, :]
    return final


# version 05
# speedup vs baseline: 1.0385x; 1.0385x over previous
import numpy as np
import ml_dtypes
import concourse.bass as bass
import concourse.mybir as mybir
from concourse.bass_utils import run_bass_kernel_spmd

F32 = mybir.dt.float32
BF16 = mybir.dt.bfloat16
I32 = mybir.dt.int32
AF = mybir.ActivationFunctionType
ALU = mybir.AluOpType
AX = mybir.AxisListType

D = 2048
S = 8192
NH = 16
DK = 128
MLP = 16384
NBLK = 16
NT = 16
OWN = 2048
EPS = 1e-5
G_C0 = 0.7978845608028654
G_C1 = 0.7978845608028654 * 0.044715
NEGBIG = -30000.0
INVSQ = float(1.0 / np.sqrt(128.0))
TWO_PI_HI = float(np.float32(6.2831855))
TWO_PI_LO = float(2 * np.pi - np.float64(np.float32(6.2831855)))
MAGIC = float(np.float32(3 * 2.0 ** 22))
KB = 1024

ENGINES_EMIT = ("sync", "tensor", "vector", "scalar")


class Builder:
    def __init__(self, sems):
        self.h = sems
        self.cnt = None
        self.active = None
        self.eng = None

    def start(self, active, eng):
        self.cnt = {k: 0 for k in ("L", "T", "P", "V", "A")}
        self.active = active
        self.eng = eng

    def dma(self, out, in_, sem="L"):
        if self.active == "sync":
            self.eng.dma_start(out=out, in_=in_).then_inc(self.h[sem], 16)
        self.cnt[sem] += 16
        return self.cnt[sem]

    def wait(self, engine, sem, val):
        if val <= 0:
            return
        if self.active == engine:
            self.eng.wait_ge(self.h[sem], val)

    def mm(self, out, lhsT, rhs, start, stop, inc=False):
        if self.active == "tensor":
            ins = self.eng.matmul(out, lhsT, rhs, start=start, stop=stop)
            if inc:
                ins.then_inc(self.h["P"], 1)
        if inc:
            self.cnt["P"] += 1
            return self.cnt["P"]
        return None

    def tr(self, out, in_, ident, inc=False):
        if self.active == "tensor":
            ins = self.eng.transpose(out, in_, ident)
            if inc:
                ins.then_inc(self.h["P"], 1)
        if inc:
            self.cnt["P"] += 1
            return self.cnt["P"]
        return None

    def ve(self, op, *a, **k):
        if self.active == "vector":
            getattr(self.eng, op)(*a, **k)

    def vdone(self, n=1):
        if self.active == "vector":
            self.eng.maybe_drain_then_inc((self.h["V"], n))
        self.cnt["V"] += n
        return self.cnt["V"]

    def ac(self, op, *a, **k):
        if self.active == "scalar":
            getattr(self.eng, op)(*a, **k)

    def adone(self, n=1):
        if self.active == "scalar":
            self.eng.maybe_drain_then_inc((self.h["A"], n))
        self.cnt["A"] += n
        return self.cnt["A"]

    def vdrain(self):
        n = self.vdone()
        self.wait("vector", "V", n)


def build_program():
    nc = bass.Bass("TRN2", target_bir_lowering=False, debug=False)

    def din(name, shape, dt=F32):
        return nc.dram_tensor(name, shape, dt, kind="ExternalInput").ap()

    x0d = din("x0", [S, D])
    wqd = din("wq", [D, D]); wkd = din("wk", [D, D])
    wvd = din("wv", [D, D]); wod = din("wo", [D, D])
    bqt = din("bqt", [NH, DK]); bkt = din("bkt", [NH, DK]); bot = din("bot", [NH, DK])
    bv_row = din("bv_row", [1, D])
    lngt = din("lngt", [NH, DK]); lnbt = din("lnbt", [NH, DK])
    gpd = din("gp", [MLP, 1024], I32); upd = din("up", [MLP, 1024], I32)
    dpd = din("dp", [D, 8192], I32)
    gsd = din("gs", [MLP, 64]); usd = din("us", [MLP, 64]); dsd = din("ds", [D, 512])
    qpos_t = din("qpos_t", [NT, 128])
    cosk = din("cosk", [128, S]); sink = din("sink", [128, S])
    qcosd = din("qcosd", [128, OWN]); qsind = din("qsind", [128, OWN])
    kidxd = din("kidxd", [128, S])
    ident_bd = din("ident_b", [128, 128], BF16)
    ident_fd = din("ident_f", [128, 128])
    ident_hd = din("ident_h", [128, 128])
    rpermTd = din("rpermT", [128, 128], BF16)
    ones_fd = din("ones_f", [1, 128])
    ones_bd = din("ones_b", [1, 128], BF16)
    qseld = din("qsel", [128, 4])
    epsd = din("epscol", [128, 1])

    outd = nc.dram_tensor("out", [OWN, D], F32, kind="ExternalOutput").ap()

    def dint(name, shape, dt):
        return nc.dram_tensor(name, shape, dt, kind="Internal").ap()

    wgT = dint("wgT", [128, 16 * MLP], BF16)
    wuT = dint("wuT", [128, 16 * MLP], BF16)
    wdT = dint("wdT", [MLP, D], BF16)
    wqT = dint("wqT", [128, 16 * D], BF16); wkT = dint("wkT", [128, 16 * D], BF16)
    wvT = dint("wvT", [128, 16 * D], BF16); woT = dint("woT", [128, 16 * D], BF16)
    x1s = dint("x1s", [S, D], F32)
    x1own = dint("x1own", [OWN, D], F32)
    x2s = dint("x2s", [OWN, D], F32)
    kall = dint("kall", [D, S], BF16)
    vall = dint("vall", [S, D], BF16)
    qall = dint("qall", [D, OWN], BF16)
    oall = dint("oall", [D, OWN], BF16)

    from contextlib import ExitStack
    stack = ExitStack()
    ec = stack.enter_context

    ARENA = 206 * KB
    arena_t = ec(nc.sbuf_tensor("arena", [128, ARENA], mybir.dt.uint8))
    arena = arena_t[:]

    def vfw(o, cols):
        return arena[:, o:o + 4 * cols].bitcast(F32)

    def vbw(o, cols):
        return arena[:, o:o + 2 * cols].bitcast(BF16)

    def viw(o, cols):
        return arena[:, o:o + 4 * cols].bitcast(I32)

    off = [0]

    def take(n, align=64):
        o = (off[0] + align - 1) // align * align
        off[0] = o + n
        assert off[0] <= ARENA, f"arena overflow {off[0]}"
        return o

    O_IDB = take(256); O_IDF = take(512); O_IDH = take(512); O_RPT = take(256)
    O_ONES = take(512)
    O_SMALL = take(512)
    O_BQ = take(64); O_BK = take(64); O_BO = take(64); O_QPT = take(64)
    O_LNGC = take(64); O_LNBC = take(64); O_ONEB = take(256)
    O_DN = take(4 * 512)
    O_T1 = take(4 * D)
    O_T2 = take(4 * D)
    O_T3 = take(4 * D)
    O_T4 = take(4 * D)
    O_T5 = take(2 * D)
    O_WBIG = take(64 * KB)
    O_FM = take(2 * 16 * 512)
    O_ACC = take(4 * 16 * 512)
    O_INT = take(8 * 2 * 512)
    O_GB = take(6 * 4 * 512)
    O_EXTRA = take(32 * KB)
    print(f"[kernel] arena used: {off[0]} bytes/partition")

    sb_idb = vbw(O_IDB, 128); sb_idf = vfw(O_IDF, 128); sb_idh = vfw(O_IDH, 128)
    sb_rpt = vbw(O_RPT, 128)
    sb_ones = vfw(O_ONES, 128)
    sb_ramp = vfw(O_SMALL, 1); sb_sign = vfw(O_SMALL + 4, 1)
    sb_eps = vfw(O_SMALL + 44, 1)
    sb_theta = vfw(O_SMALL + 8, 1)
    sb_mu = vfw(O_SMALL + 12, 1); sb_var = vfw(O_SMALL + 16, 1)
    sb_sd = vfw(O_SMALL + 20, 1); sb_nm = vfw(O_SMALL + 24, 1)
    sb_rm = vfw(O_SMALL + 28, 1); sb_l = vfw(O_SMALL + 32, 1)
    sb_rl = vfw(O_SMALL + 36, 1); sb_t0 = vfw(O_SMALL + 40, 1)
    sb_qsel = vfw(O_SMALL + 48, 4)
    sb_bq = vfw(O_BQ, NH); sb_bk = vfw(O_BK, NH); sb_bo = vfw(O_BO, NH)
    sb_qpt = vfw(O_QPT, NT)
    sb_lngc = vfw(O_LNGC, NH); sb_lnbc = vfw(O_LNBC, NH)
    sb_oneb = vbw(O_ONEB, 128)
    v_dn = vfw(O_DN, 512)
    v_xc = vfw(O_T1, D); v_sq = vfw(O_T2, D)
    v_xt = vfw(O_T3, D); v_ut = vfw(O_T4, D); v_ztm = vbw(O_T5, D)
    v_fm = vbw(O_FM, 16 * 512)
    v_acc = vfw(O_ACC, 16 * 512)

    v_kv1 = vfw(O_EXTRA + 16 * KB, 512)
    v_kv2 = vfw(O_EXTRA + 18 * KB, 512)
    v_kv3 = vbw(O_EXTRA + 20 * KB, 512)
    v_kv4 = vbw(O_EXTRA + 21 * KB, 512)
    v_cosb = vfw(O_EXTRA + 22 * KB, 512)
    v_sinb = vfw(O_EXTRA + 24 * KB, 512)
    v_bvb = vbw(O_EXTRA + 26 * KB, D)

    def gbuf(i):
        return vfw(O_GB + i * 2 * KB, 512)

    def intbuf(i):
        return vbw(O_INT + i * 2 * 512, 512)

    v_row = vfw(O_EXTRA, S)
    v_prw = vbw(O_ACC, S)
    v_spt = vbw(O_ACC + 16 * KB, S)
    v_ki = v_dn
    v_msk_off = O_GB
    v_rlb = vfw(O_FM, 128)
    v_rlr = vfw(O_FM + 2 * KB, 128)
    v_oo = vbw(O_FM + 4 * KB, 128)
    v_bl0 = vfw(O_EXTRA, D)
    v_bl1 = vfw(O_EXTRA + 8 * KB, D)
    r_bv = vfw(O_EXTRA, D)
    r_qp = vfw(O_EXTRA + 8 * KB, OWN)
    r_ki = vfw(O_FM, S)

    DQ = O_WBIG
    dq_pk = viw(DQ, 1024)
    dq_sc = vfw(DQ + 4 * KB, 64)
    dq_sce = vfw(DQ + 5 * KB, 1024)
    dq_c = viw(DQ + 9 * KB, 1024)
    dq_t = viw(DQ + 13 * KB, 1024)
    dq_dd = vfw(DQ + 17 * KB, 1024)
    dq_ff = vfw(DQ + 21 * KB, 1024)
    DQ_PL = DQ + 25 * KB

    def dq_pl(i):
        return vbw(DQ_PL + i * 2 * KB, 1024)

    dq_wt = vbw(DQ + 57 * KB, 1024)
    aw_row = vfw(DQ, D)
    aw_bf = vbw(DQ + 8 * KB, D)
    aw_wt = vbw(DQ + 12 * KB, 512)
    tb_pos = vfw(DQ + 16 * KB, 512)
    tb_fr = vfw(DQ + 18 * KB, 512)
    tb_n = vfw(DQ + 20 * KB, 512)
    tb_red = vfw(DQ + 22 * KB, 512)
    tb_tb = vfw(DQ + 24 * KB, 512)

    O_WG = [O_WBIG, O_WBIG + 8 * KB]
    O_WU = [O_WBIG + 16 * KB, O_WBIG + 24 * KB]
    O_WD = [O_WBIG + 32 * KB + i * 4 * KB for i in range(8)]
    O_WK = O_WBIG
    O_WV = O_WBIG + 16 * KB
    O_WQ = [O_WBIG, O_WBIG + 32 * KB]
    O_K2 = O_WBIG
    O_V2 = O_WBIG + 16 * KB
    O_Q2 = O_WBIG + 32 * KB
    O_WO = O_WBIG
    O_OSB = O_INT

    ps = [ec(nc.psum_tensor(f"ps{i}", [128, 512], F32)) for i in range(8)]

    def psb(i, cols=512):
        return ps[i][:].bitcast(BF16)[:, 0:cols]

    sems = {k: ec(nc.semaphore(k)) for k in ("L", "T", "P", "V", "A")}
    blk = ec(nc.Block())
    b = Builder(sems)

    def flow():
        b.dma(sb_idb, ident_bd)
        b.dma(sb_idf, ident_fd)
        b.dma(sb_idh, ident_hd)
        b.dma(sb_rpt, rpermTd)
        b.dma(sb_ones[0:1, :], ones_fd)
        b.dma(sb_qsel, qseld)
        b.dma(sb_eps, epsd)
        b.dma(sb_oneb[0:1, :], ones_bd)
        for h in range(NH):
            b.dma(sb_bq[:, h:h + 1], bqt[h:h + 1, :].rearrange("o p -> p o"))
            b.dma(sb_bk[:, h:h + 1], bkt[h:h + 1, :].rearrange("o p -> p o"))
            b.dma(sb_bo[:, h:h + 1], bot[h:h + 1, :].rearrange("o p -> p o"))
            b.dma(sb_lngc[:, h:h + 1], lngt[h:h + 1, :].rearrange("o p -> p o"))
            b.dma(sb_lnbc[:, h:h + 1], lnbt[h:h + 1, :].rearrange("o p -> p o"))
        for i in range(NT):
            b.dma(sb_qpt[:, i:i + 1], qpos_t[i:i + 1, :].rearrange("o p -> p o"))
        Lc = b.dma(r_bv[0:1, :], bv_row)

        b.wait("vector", "L", Lc)
        b.ve("tensor_copy", v_bvb[0:1, :], r_bv[0:1, :])
        b.vdone()
        b.wait("tensor", "L", Lc)

        for (wsrc, wdst) in ((wqd, wqT), (wkd, wkT), (wvd, wvT), (wod, woT)):
            for mc in range(16):
                b.wait("sync", "V", b.cnt["V"])
                lm = b.dma(aw_row, wsrc[mc * 128:(mc + 1) * 128, :])
                b.wait("vector", "L", lm)
                b.wait("vector", "T", b.cnt["T"])
                b.ve("tensor_copy", aw_bf, aw_row)
                vm = b.vdone()
                b.wait("tensor", "V", vm)
                for g in range(4):
                    pm = None
                    for q in range(4):
                        kc = g * 4 + q
                        pm = b.tr(psb(2)[:, q * 128:(q + 1) * 128],
                                  aw_bf[:, kc * 128:(kc + 1) * 128], sb_idb,
                                  inc=(q == 3))
                    b.wait("scalar", "P", pm)
                    b.wait("scalar", "T", b.cnt["T"])
                    b.ac("copy", aw_wt, psb(2))
                    am = b.adone()
                    b.wait("tensor", "A", am)
                    b.wait("sync", "A", am)
                    wv4 = wdst.rearrange("p (mg kc m) -> p mg kc m", kc=16, m=512)
                    for q in range(4):
                        kc = g * 4 + q
                        b.dma(wv4[:, mc // 4, kc,
                                  (mc % 4) * 128:(mc % 4) * 128 + 128],
                              aw_wt[:, q * 128:(q + 1) * 128], sem="T")

        state = {"pl_used": 0}

        def dequant(packed, scales, wdstT, NROW, JW, gmajor=False):
            nchunks = NROW // 128
            nslices = JW // 1024
            for g in range(nchunks // 8):
                for js in range(nslices):
                    vplanes = None
                    for s8 in range(8):
                        n0 = (g * 8 + s8) * 128
                        b.wait("sync", "V", b.cnt["V"])
                        lm = b.dma(dq_pk, packed[n0:n0 + 128, js * 1024:(js + 1) * 1024])
                        ls = b.dma(dq_sc, scales[n0:n0 + 128, js * 64:(js + 1) * 64])
                        b.wait("vector", "L", ls)
                        if state["pl_used"]:
                            b.wait("vector", "P", state["pl_used"])
                        b.ve("tensor_copy",
                             dq_sce.rearrange("p (bb r) -> p bb r", r=16),
                             dq_sc.unsqueeze(2).broadcast_to([128, 64, 16]))
                        for nib in range(2):
                            if nib == 0:
                                b.ve("tensor_scalar", dq_c, dq_pk, 15, None,
                                     ALU.bitwise_and)
                            else:
                                b.ve("tensor_scalar", dq_c, dq_pk, 4, None,
                                     ALU.logical_shift_right)
                            b.ve("tensor_scalar", dq_t, dq_c, 7, 22,
                                 ALU.bitwise_and, ALU.logical_shift_left)
                            b.ve("tensor_scalar", dq_t, dq_t, 0x3F000000, None, ALU.add)
                            u = dq_t.bitcast(F32)
                            b.ve("tensor_scalar", dq_dd, u, 1.0, 0.0,
                                 ALU.subtract, ALU.min)
                            b.ve("tensor_tensor", dq_ff, u, dq_dd, ALU.add)
                            b.ve("tensor_scalar", dq_c, dq_c, 8, 28,
                                 ALU.bitwise_and, ALU.logical_shift_left)
                            b.ve("tensor_tensor", dq_ff.bitcast(I32),
                                 dq_ff.bitcast(I32), dq_c, ALU.bitwise_or)
                            b.ve("tensor_tensor", dq_pl(s8 * 2 + nib), dq_ff,
                                 dq_sce, ALU.mult)
                        vplanes = b.vdone()
                    pm = None
                    for nib in range(2):
                        for jc in range(8):
                            b.wait("tensor", "V", vplanes)
                            for s8 in range(8):
                                pm = b.tr(psb(3, 1024)[:, s8 * 128:(s8 + 1) * 128],
                                          dq_pl(s8 * 2 + nib)[:, jc * 128:(jc + 1) * 128],
                                          sb_idb, inc=(s8 == 7))
                            b.wait("scalar", "P", pm)
                            b.wait("scalar", "T", b.cnt["T"])
                            b.ac("copy", dq_wt, psb(3, 1024))
                            am = b.adone()
                            b.wait("tensor", "A", am)
                            b.wait("sync", "A", am)
                            if gmajor:
                                gv = wdstT.rearrange(
                                    "(r64 two) (gg kc n) -> two r64 gg kc n",
                                    two=2, kc=16, n=256)
                                for half in range(2):
                                    b.dma(gv[nib, :, g * 4:(g + 1) * 4,
                                             2 * jc + half, :],
                                          dq_wt[half * 64:(half + 1) * 64, :]
                                          .rearrange("p (gg n) -> p gg n", n=256),
                                          sem="T")
                            else:
                                jrow = js * 1024 + jc * 128
                                b.dma(wdstT.rearrange("(k2 two) n -> two k2 n", two=2)
                                      [nib, jrow:jrow + 128, g * 1024:(g + 1) * 1024],
                                      dq_wt, sem="T")
                    state["pl_used"] = pm

        dequant(gpd, gsd, wgT, MLP, 1024, gmajor=True)
        dequant(upd, usd, wuT, MLP, 1024, gmajor=True)
        dequant(dpd, dsd, wdT, D, 8192)
        Tw = b.cnt["T"]

        def layernorm(xt, outd_ap, gamma=None, beta=None):
            b.ve("reduce_sum", sb_mu, xt, AX.X)
            b.vdrain()
            b.ve("tensor_scalar", sb_mu, sb_mu, 1.0 / D, None, ALU.mult)
            b.vdrain()
            b.ve("tensor_scalar", v_xc, xt, sb_mu[:, 0:1], None, ALU.subtract)
            vm = b.vdone()
            b.wait("scalar", "V", vm)
            b.ac("activation", v_sq, v_xc, AF.Square, bias=0.0, scale=1.0)
            am = b.adone()
            b.wait("vector", "A", am)
            b.ve("reduce_sum", sb_var, v_sq, AX.X)
            vm = b.vdone()
            b.wait("scalar", "V", vm)
            b.ac("activation", sb_sd, sb_var, AF.Sqrt, bias=sb_eps[:, 0:1], scale=1.0 / D)
            am = b.adone()
            b.wait("vector", "A", am)
            b.ve("reciprocal", sb_sd, sb_sd)
            b.vdrain()
            if gamma is None:
                b.ve("tensor_scalar", outd_ap, v_xc, sb_sd[:, 0:1], None, ALU.mult)
            else:
                b.ve("tensor_scalar", v_xc, v_xc, sb_sd[:, 0:1], None, ALU.mult)
                b.ve("tensor_tensor", v_sq, v_xc, gamma, ALU.mult)
                b.ve("tensor_tensor", outd_ap, v_sq, beta, ALU.add)
            return b.vdone()

        def tm_to_fm(src_bf, sub, vready, gb=False, fm_cols=512):
            b.wait("tensor", "V", vready)
            am = None
            for g in range(4):
                pm = None
                for q in range(4):
                    kc = g * 4 + q
                    pm = b.tr(psb(6)[:, q * 128:(q + 1) * 128],
                              src_bf[:, kc * 128:(kc + 1) * 128], sb_idb,
                              inc=(q == 3))
                b.wait("scalar", "P", pm)
                if gb:
                    for q in range(4):
                        kc = g * 4 + q
                        b.ac("activation",
                             v_fm[:, kc * fm_cols + sub * 128:
                                  kc * fm_cols + sub * 128 + 128],
                             psb(6)[:, q * 128:(q + 1) * 128], AF.Identity,
                             bias=sb_lnbc[:, kc:kc + 1], scale=sb_lngc[:, kc:kc + 1])
                else:
                    b.ac("copy",
                         v_fm.rearrange("p (kc t) -> p kc t", t=fm_cols)
                         [:, g * 4:(g + 1) * 4, sub * 128:sub * 128 + 128],
                         psb(6).rearrange("p (q t) -> p q t", t=128))
                am = b.adone()
                b.wait("tensor", "A", am)
            return am

        def emit_ffn(fm_ready_a, guard_P=0):
            NWG = 2
            NGW = 128 // NWG
            lw_wg = [0] * (NGW + 2)
            lw_wd = [0] * 129
            pm_grp = [0] * (NGW + 1)
            vm_inter = [0] * 130
            am_down = [0] * 17
            pm_down = [0] * 17

            def wgv(g):
                return vbw(O_WG[g % 2], 16 * 256)

            def wuv(g):
                return vbw(O_WU[g % 2], 16 * 256)

            def wdv(c):
                return vbw(O_WD[c % 8], 2048)

            def load_wgu(g):
                b.dma(wgv(g), wgT[:, g * 4096:(g + 1) * 4096])
                return b.dma(wuv(g), wuT[:, g * 4096:(g + 1) * 4096])

            b.wait("sync", "P", guard_P)
            lw_wg[0] = load_wgu(0)
            lw_wg[1] = load_wgu(1)
            for c in range(8):
                lw_wd[c] = b.dma(wdv(c), wdT[c * 128:(c + 1) * 128, :])

            b.wait("tensor", "A", fm_ready_a)
            for g in range(NGW):
                b.wait("tensor", "L", lw_wg[g])
                for q in range(NWG):
                    c = g * NWG + q
                    rot = c % 2
                    pgp = ps[rot]
                    pup = ps[2 + rot]
                    if c >= 2:
                        b.wait("tensor", "V", vm_inter[c - 2])
                    pmg = None
                    for kc in range(16):
                        pmg = b.mm(pgp[:],
                                   wgv(g)[:, kc * 256 + q * 128: kc * 256 + q * 128 + 128],
                                   v_fm[:, kc * 512:(kc + 1) * 512],
                                   start=(kc == 0), stop=(kc == 15), inc=(kc == 15))
                    pmu = None
                    for kc in range(16):
                        pmu = b.mm(pup[:],
                                   wuv(g)[:, kc * 256 + q * 128: kc * 256 + q * 128 + 128],
                                   v_fm[:, kc * 512:(kc + 1) * 512],
                                   start=(kc == 0), stop=(kc == 15), inc=(kc == 15))
                    gs_ = gbuf(0); gw_ = gbuf(1); gi_ = gbuf(2)
                    gt_ = gbuf(3); gu_ = gbuf(4); ga_ = gbuf(5)
                    b.wait("scalar", "P", pmg)
                    if c >= 1:
                        b.wait("scalar", "V", vm_inter[c - 1])
                    b.ac("activation", gs_, pgp[:], AF.Square, bias=0.0, scale=1.0)
                    b.wait("scalar", "P", pmu)
                    b.ac("activation", gu_, pup[:], AF.Copy, bias=0.0, scale=0.5)
                    am1 = b.adone()
                    b.wait("vector", "A", am1)
                    b.ve("tensor_scalar", gw_, gs_, G_C1, G_C0, ALU.mult, ALU.add)
                    b.ve("tensor_tensor", gi_, gw_, pgp[:], ALU.mult)
                    vm1 = b.vdone()
                    b.wait("scalar", "V", vm1)
                    b.ac("activation", gt_, gi_, AF.Tanh, bias=0.0, scale=1.0)
                    am2 = b.adone()
                    b.wait("vector", "A", am2)
                    b.ve("scalar_tensor_tensor", ga_, gt_, 1.0, pgp[:],
                         ALU.add, ALU.mult)
                    b.ve("tensor_tensor", intbuf(c % 8), ga_, gu_, ALU.mult)
                    vm_inter[c] = b.vdone()
                pm_grp[g] = b.cnt["P"]
                if g + 2 < NGW:
                    b.wait("sync", "P", pm_grp[g])
                    lw_wg[g + 2] = load_wgu(g + 2)
                if (g + 1) * NWG % 8 == 0:
                    grp = ((g + 1) * NWG) // 8 - 1
                    c_hi = grp * 8 + 7
                    b.wait("tensor", "V", vm_inter[c_hi])
                    b.wait("tensor", "L", lw_wd[c_hi])
                    if grp >= 2:
                        b.wait("tensor", "A", am_down[grp - 2])
                    pd = ps[4 + grp % 2]
                    for j in range(16):
                        pm = None
                        for cc in range(8):
                            c2 = grp * 8 + cc
                            pm = b.mm(pd[:], wdv(c2)[:, j * 128:(j + 1) * 128],
                                      intbuf(c2 % 8),
                                      start=(cc == 0), stop=(cc == 7), inc=(cc == 7))
                        b.wait("scalar", "P", pm)
                        if grp > 0 or j > 1:
                            b.wait("scalar", "V", b.cnt["V"])
                        b.ac("copy", v_dn, pd[:])
                        am = b.adone()
                        b.wait("vector", "A", am)
                        if grp == 0:
                            b.ve("tensor_copy", v_acc[:, j * 512:(j + 1) * 512], v_dn)
                        else:
                            b.ve("tensor_tensor", v_acc[:, j * 512:(j + 1) * 512],
                                 v_acc[:, j * 512:(j + 1) * 512], v_dn, ALU.add)
                        vmj = b.vdone()
                        b.wait("tensor", "A", am)
                    am_down[grp] = b.cnt["A"]
                    pm_down[grp] = b.cnt["P"]
                    if c_hi + 1 < 128:
                        b.wait("sync", "P", pm_down[grp])
                        for cpre in range(c_hi + 1, min(c_hi + 9, 128)):
                            lw_wd[cpre] = b.dma(wdv(cpre),
                                                wdT[cpre * 128:(cpre + 1) * 128, :])
            return b.cnt["P"], b.cnt["V"]

        fin_vm = {5: 0, 6: 0, 7: 0}
        b.wait("sync", "T", Tw)
        vk_used = 0
        vv_used = 0
        for bb in range(NBLK):
            b.wait("sync", "V", vk_used)
            b.dma(v_cosb, cosk[:, bb * 512:(bb + 1) * 512])
            lcs = b.dma(v_sinb, sink[:, bb * 512:(bb + 1) * 512])
            for sub in range(4):
                b.wait("sync", "V", b.cnt["V"])
                lm = b.dma(v_xt, x0d[bb * 512 + sub * 128: bb * 512 + sub * 128 + 128, :])
                b.wait("vector", "L", lm)
                b.wait("vector", "A", b.cnt["A"])
                vml = layernorm(v_xt, v_ztm)
                fm_a = tm_to_fm(v_ztm, sub, vml)
            pffn, vffn = emit_ffn(fm_a, b.cnt["P"])
            b.wait("tensor", "V", vffn)
            b.wait("tensor", "A", b.cnt["A"])
            for sub in range(4):
                b.wait("sync", "V", b.cnt["V"])
                b.wait("sync", "T", b.cnt["T"])
                lm = b.dma(v_xt, x0d[bb * 512 + sub * 128: bb * 512 + sub * 128 + 128, :])
                for jg in range(4):
                    bk = 7 if jg % 2 == 0 else 5
                    pp = ps[bk]
                    b.wait("tensor", "V", fin_vm[bk])
                    pm = None
                    for q in range(4):
                        j = jg * 4 + q
                        pm = b.tr(pp[:, q * 128:(q + 1) * 128],
                                  v_acc.rearrange("p (j t) -> p j t", t=512)
                                  [:, j, sub * 128:sub * 128 + 128],
                                  sb_idh, inc=(q == 3))
                    b.wait("vector", "P", pm)
                    b.wait("vector", "L", lm)
                    b.ve("scalar_tensor_tensor", v_xt[:, jg * 512:(jg + 1) * 512],
                         pp[:], 0.5, v_xt[:, jg * 512:(jg + 1) * 512],
                         ALU.mult, ALU.add)
                    fin_vm[bk] = b.vdone()
                b.wait("sync", "V", b.cnt["V"])
                b.dma(x1s[bb * 512 + sub * 128: bb * 512 + sub * 128 + 128, :],
                      v_xt, sem="T")
                layernorm(v_xt, v_ut)
                vz = layernorm(v_ut, v_ztm)
                fm_a = tm_to_fm(v_ztm, sub, vz, gb=True)
            sWK = vbw(O_WK, 16 * 512)
            sWV = vbw(O_WV, 16 * 512)
            for hg in range(4):
                b.wait("sync", "P", b.cnt["P"])
                lwk = b.dma(sWK, wkT[:, hg * 8192:(hg + 1) * 8192])
                for hh in range(4):
                    h = hg * 4 + hh
                    b.wait("tensor", "L", lwk)
                    b.wait("tensor", "A", fm_a)
                    if vk_used:
                        b.wait("tensor", "V", vk_used)
                    pm = None
                    for kc in range(16):
                        pm = b.mm(ps[7][:],
                                  sWK[:, kc * 512 + hh * 128: kc * 512 + hh * 128 + 128],
                                  v_fm[:, kc * 512:(kc + 1) * 512],
                                  start=(kc == 0), stop=(kc == 15), inc=(kc == 15))
                    b.wait("scalar", "P", pm)
                    b.wait("scalar", "V", vk_used if vk_used else vffn)
                    b.ac("activation", v_kv1, ps[7][:], AF.Identity,
                         bias=sb_bk[:, h:h + 1], scale=1.0)
                    b.ac("copy", v_kv3, v_kv1)
                    am = b.adone()
                    b.wait("tensor", "A", am)
                    pmr = b.mm(ps[4][:], sb_rpt, v_kv3, start=True, stop=True, inc=True)
                    b.wait("vector", "A", am)
                    b.wait("vector", "P", pmr)
                    b.wait("vector", "L", lcs)
                    b.wait("vector", "T", b.cnt["T"])
                    b.ve("tensor_tensor", v_kv2, v_kv1, v_cosb, ALU.mult)
                    b.ve("tensor_tensor", v_kv1, ps[4][:], v_sinb, ALU.mult)
                    b.ve("tensor_tensor", v_kv4, v_kv2, v_kv1, ALU.add)
                    vk_used = b.vdone()
                    b.wait("sync", "V", vk_used)
                    b.dma(kall[h * 128:(h + 1) * 128, bb * 512:(bb + 1) * 512],
                          v_kv4, sem="T")
            for dv in range(4):
                b.wait("sync", "P", b.cnt["P"])
                lwv = b.dma(sWV, wvT[:, dv * 8192:(dv + 1) * 8192])
                for sub in range(4):
                    b.wait("tensor", "L", lwv)
                    if vv_used:
                        b.wait("tensor", "V", vv_used)
                    pm = None
                    for kc in range(16):
                        b.mm(ps[5][:],
                             v_fm[:, kc * 512 + sub * 128: kc * 512 + sub * 128 + 128],
                             sWV[:, kc * 512:(kc + 1) * 512],
                             start=(kc == 0), stop=False)
                    pm = b.mm(ps[5][:], sb_oneb[0:1, :],
                              v_bvb[0:1, dv * 512:(dv + 1) * 512],
                              start=False, stop=True, inc=True)
                    b.wait("vector", "P", pm)
                    b.wait("vector", "T", b.cnt["T"])
                    b.ve("tensor_copy", v_kv4, ps[5][:])
                    vv_used = b.vdone()
                    b.wait("sync", "V", vv_used)
                    b.dma(vall[bb * 512 + sub * 128: bb * 512 + sub * 128 + 128,
                               dv * 512:(dv + 1) * 512], v_kv4, sem="T")
        T_p1 = b.cnt["T"]

        b.wait("sync", "T", T_p1)
        sWQ0 = vbw(O_WQ[0], 16 * 1024)
        sWQ1 = vbw(O_WQ[1], 16 * 1024)
        vq_used = 0
        for i in range(NT):
            b.wait("vector", "T", b.cnt["T"])
            for j in range(4):
                bl = v_bl0 if j % 2 == 0 else v_bl1
                b.wait("sync", "V", b.cnt["V"] if j >= 2 else 0)
                lm = b.dma(bl, x1s[i * 512 + j * 128: i * 512 + j * 128 + 128, :])
                b.wait("vector", "L", lm)
                if j == 0:
                    b.ve("tensor_scalar", v_xt, bl, sb_qsel[:, 0:1], None, ALU.mult)
                else:
                    b.ve("tensor_scalar", v_ut, bl, sb_qsel[:, j:j + 1], None, ALU.mult)
                    b.ve("tensor_tensor", v_xt, v_xt, v_ut, ALU.add)
                b.vdone()
            vm = b.cnt["V"]
            b.wait("sync", "V", vm)
            b.dma(x1own[i * 128:(i + 1) * 128, :], v_xt, sem="T")
            layernorm(v_xt, v_ut)
            vz2 = layernorm(v_ut, v_ztm)
            zf = v_fm
            b.wait("tensor", "V", vz2)
            zready = None
            for g in range(4):
                pm = None
                for q in range(4):
                    kc = g * 4 + q
                    pm = b.tr(psb(6)[:, q * 128:(q + 1) * 128],
                              v_ztm[:, kc * 128:(kc + 1) * 128], sb_idb, inc=(q == 3))
                b.wait("scalar", "P", pm)
                for q in range(4):
                    kc = g * 4 + q
                    b.ac("activation", zf[:, kc * 128:(kc + 1) * 128],
                         psb(6)[:, q * 128:(q + 1) * 128], AF.Identity,
                         bias=sb_lnbc[:, kc:kc + 1], scale=sb_lngc[:, kc:kc + 1])
                zready = b.adone()
                b.wait("tensor", "A", zready)
            b.wait("sync", "V", vq_used)
            b.wait("sync", "P", b.cnt["P"])
            lq = b.dma(v_cosb[:, 0:128], qcosd[:, i * 128:(i + 1) * 128])
            lq = b.dma(v_sinb[:, 0:128], qsind[:, i * 128:(i + 1) * 128])
            lwq = b.dma(sWQ0, wqT[:, 0:16384])
            lwq1 = b.dma(sWQ1, wqT[:, 16384:32768])
            for h in range(NH):
                sw = sWQ0 if h < 8 else sWQ1
                hh = h % 8
                ho = (hh // 4) * 8192 + (hh % 4) * 128
                b.wait("tensor", "L", lwq if h < 8 else lwq1)
                b.wait("tensor", "A", zready)
                if vq_used:
                    b.wait("tensor", "V", vq_used)
                pm = None
                for kc in range(16):
                    pm = b.mm(ps[7][:, 0:128],
                              sw[:, ho + kc * 512: ho + kc * 512 + 128],
                              zf[:, kc * 128:(kc + 1) * 128],
                              start=(kc == 0), stop=(kc == 15), inc=(kc == 15))
                b.wait("scalar", "P", pm)
                if vq_used:
                    b.wait("scalar", "V", vq_used)
                b.ac("activation", v_kv1[:, 0:128], ps[7][:, 0:128], AF.Identity,
                     bias=sb_bq[:, h:h + 1], scale=1.0)
                b.ac("copy", v_kv3[:, 0:128], v_kv1[:, 0:128])
                am = b.adone()
                b.wait("tensor", "A", am)
                pmr = b.mm(ps[4][:, 0:128], sb_rpt, v_kv3[:, 0:128],
                           start=True, stop=True, inc=True)
                b.wait("vector", "A", am)
                b.wait("vector", "P", pmr)
                b.wait("vector", "L", lq)
                b.wait("vector", "T", b.cnt["T"])
                b.ve("tensor_tensor", v_kv2[:, 0:128], v_kv1[:, 0:128],
                     v_cosb[:, 0:128], ALU.mult)
                b.ve("tensor_tensor", v_kv1[:, 0:128], ps[4][:, 0:128],
                     v_sinb[:, 0:128], ALU.mult)
                b.ve("tensor_tensor", v_kv4[:, 0:128], v_kv2[:, 0:128],
                     v_kv1[:, 0:128], ALU.add)
                vq_used = b.vdone()
                b.wait("sync", "V", vq_used)
                b.dma(qall[h * 128:(h + 1) * 128, i * 128:(i + 1) * 128],
                      v_kv4[:, 0:128], sem="T")
        T_p2a = b.cnt["T"]

        sK = vbw(O_K2, S)
        sV = vbw(O_V2, S)
        sQ = vbw(O_Q2, OWN)
        b.wait("sync", "T", T_p2a)
        for h in range(NH):
            b.wait("sync", "P", b.cnt["P"])
            lK = b.dma(sK, kall[h * 128:(h + 1) * 128, :])
            lV = b.dma(sV.rearrange("p (jt dd) -> p jt dd", dd=128),
                       vall.rearrange("(jt p) dd -> p jt dd", p=128)
                       [:, :, h * 128:(h + 1) * 128])
            lQ = b.dma(sQ, qall[h * 128:(h + 1) * 128, :])
            for i in range(NT):
                nch = i + 1
                b.wait("tensor", "L", lQ)
                b.wait("tensor", "L", lK)
                b.wait("tensor", "A", b.cnt["A"])
                am_evac = [0] * nch
                for c in range(nch):
                    pss = ps[c % 2]
                    if c >= 2:
                        b.wait("tensor", "A", am_evac[c - 2])
                    pm = b.mm(pss[:], sQ[:, i * 128:(i + 1) * 128],
                              sK[:, c * 512:(c + 1) * 512],
                              start=True, stop=True, inc=True)
                    b.wait("scalar", "P", pm)
                    b.ac("copy", v_row[:, c * 512:(c + 1) * 512], pss[:])
                    am_evac[c] = b.adone()
                lk = b.dma(v_ki, kidxd[:, i * 512:(i + 1) * 512])
                b.wait("vector", "L", lk)
                b.wait("vector", "A", am_evac[nch - 1])
                v_msk = vfw(O_GB, 512)
                b.ve("tensor_scalar", v_msk, v_ki, sb_qpt[:, i:i + 1], None, ALU.is_gt)
                b.ve("scalar_tensor_tensor", v_row[:, i * 512:(i + 1) * 512],
                     v_msk, NEGBIG, v_row[:, i * 512:(i + 1) * 512],
                     ALU.mult, ALU.add)
                b.ve("reduce_max", sb_rm, v_row[:, 0:nch * 512], AX.X)
                b.vdrain()
                b.ve("tensor_scalar", sb_nm, sb_rm, -1.0, None, ALU.mult)
                vm = b.vdone()
                b.wait("scalar", "V", vm)
                b.ac("activation", v_prw[:, 0:nch * 512], v_row[:, 0:nch * 512],
                     AF.Exp, bias=sb_nm[:, 0:1], scale=1.0, accum_out=sb_l)
                am_exp = b.adone()
                b.wait("vector", "A", am_exp)
                b.ve("reciprocal", sb_rl, sb_l)
                vrl = b.vdone()
                b.wait("tensor", "V", vrl)
                pm = b.tr(ps[4][0:1, 0:128], sb_rl, sb_idf, inc=True)
                b.wait("scalar", "P", pm)
                b.ac("copy", v_rlr[0:1, :], ps[4][0:1, 0:128])
                am = b.adone()
                b.wait("tensor", "A", am)
                pm = b.mm(ps[5][:, 0:128], sb_ones[0:1, :], v_rlr[0:1, :],
                          start=True, stop=True, inc=True)
                b.wait("scalar", "P", pm)
                b.ac("copy", v_rlb, ps[5][:, 0:128])
                am_rlb = b.adone()
                for c in range(nch):
                    if c > 0:
                        b.wait("tensor", "A", b.cnt["A"])
                    pm = None
                    for half in range(2):
                        for jt in range(2):
                            sl = c * 4 + half * 2 + jt
                            pm = b.tr(psb(2)[:, (half * 2 + jt) * 128:
                                             (half * 2 + jt + 1) * 128],
                                      v_prw[:, sl * 128:(sl + 1) * 128],
                                      sb_idb, inc=(half == 1 and jt == 1))
                    b.wait("scalar", "P", pm)
                    b.ac("copy", v_spt[:, c * 512:(c + 1) * 512], psb(2))
                    b.adone()
                am_pt = b.cnt["A"]
                b.wait("tensor", "A", am_pt)
                b.wait("tensor", "L", lV)
                first = True
                pmO = None
                for c in range(nch):
                    for jt in range(4):
                        sl = c * 4 + jt
                        pmO = b.mm(ps[3][:, 0:128],
                                   sV[:, sl * 128:(sl + 1) * 128],
                                   v_spt[:, sl * 128:(sl + 1) * 128],
                                   start=first, stop=(c == nch - 1 and jt == 3),
                                   inc=(c == nch - 1 and jt == 3))
                        first = False
                b.wait("vector", "P", pmO)
                b.wait("vector", "A", am_rlb)
                b.wait("vector", "T", b.cnt["T"])
                b.ve("tensor_tensor", v_oo, ps[3][:, 0:128], v_rlb, ALU.mult)
                vo = b.vdone()
                b.wait("sync", "V", vo)
                b.dma(oall[h * 128:(h + 1) * 128, i * 128:(i + 1) * 128],
                      v_oo, sem="T")
        T_p2b = b.cnt["T"]

        sWo = vbw(O_WO, 16 * 2048)
        sOsb = vbw(O_OSB, 16 * 512)
        b.wait("sync", "T", T_p2b)
        for gq in range(4):
            b.wait("sync", "P", b.cnt["P"])
            lwo = b.dma(sWo, woT[:, :])
            lo = b.dma(sOsb.rearrange("p (kc t) -> p kc t", t=512),
                       oall.rearrange("(kc p) t -> p kc t", p=128)
                       [:, :, gq * 512:(gq + 1) * 512])
            am_dj = [0] * 16
            for j in range(16):
                pdj = ps[j % 2]
                jo = (j // 4) * 8192 + (j % 4) * 128
                b.wait("tensor", "L", lwo)
                b.wait("tensor", "L", lo)
                if j >= 2:
                    b.wait("tensor", "A", am_dj[j - 2])
                pm = None
                for kc in range(16):
                    pm = b.mm(pdj[:],
                              sWo[:, jo + kc * 512: jo + kc * 512 + 128],
                              sOsb[:, kc * 512:(kc + 1) * 512],
                              start=(kc == 0), stop=(kc == 15), inc=(kc == 15))
                b.wait("scalar", "P", pm)
                b.ac("activation", v_acc[:, j * 512:(j + 1) * 512], pdj[:], AF.Identity,
                     bias=sb_bo[:, j:j + 1], scale=1.0)
                am_dj[j] = b.adone()
            am_delta = b.cnt["A"]
            for sub in range(4):
                b.wait("sync", "V", b.cnt["V"])
                b.wait("sync", "T", b.cnt["T"])
                lm = b.dma(v_xt, x1own[gq * 512 + sub * 128: gq * 512 + sub * 128 + 128, :])
                b.wait("vector", "L", lm)
                layernorm(v_xt, v_ut)
                b.ve("tensor_tensor", v_xt, v_xt, v_ut, ALU.add)
                vxu = b.vdone()
                b.wait("tensor", "A", am_delta)
                for jg in range(4):
                    bk = 6 if jg % 2 == 0 else 7
                    pp = ps[bk]
                    b.wait("tensor", "V", fin_vm[bk])
                    pm = None
                    for q in range(4):
                        j = jg * 4 + q
                        pm = b.tr(pp[:, q * 128:(q + 1) * 128],
                                  v_acc.rearrange("p (j t) -> p j t", t=512)
                                  [:, j, sub * 128:sub * 128 + 128],
                                  sb_idf, inc=(q == 3))
                    b.wait("vector", "P", pm)
                    b.wait("vector", "V", vxu)
                    b.ve("tensor_tensor", v_xt[:, jg * 512:(jg + 1) * 512],
                         v_xt[:, jg * 512:(jg + 1) * 512], pp[:], ALU.add)
                    fin_vm[bk] = b.vdone()
                b.wait("sync", "V", b.cnt["V"])
                b.dma(x2s[gq * 512 + sub * 128: gq * 512 + sub * 128 + 128, :],
                      v_xt, sem="T")
                vml = layernorm(v_xt, v_ztm)
                fm_a = tm_to_fm(v_ztm, sub, vml)
            pffn, vffn = emit_ffn(fm_a, b.cnt["P"])
            b.wait("tensor", "V", vffn)
            b.wait("tensor", "A", b.cnt["A"])
            Tx2 = b.cnt["T"]
            for sub in range(4):
                b.wait("sync", "T", Tx2)
                b.wait("sync", "V", b.cnt["V"])
                lm = b.dma(v_xt, x2s[gq * 512 + sub * 128: gq * 512 + sub * 128 + 128, :])
                b.wait("vector", "L", lm)
                for jg in range(4):
                    bk = 6 if jg % 2 == 0 else 7
                    pp = ps[bk]
                    b.wait("tensor", "V", fin_vm[bk])
                    pm = None
                    for q in range(4):
                        j = jg * 4 + q
                        pm = b.tr(pp[:, q * 128:(q + 1) * 128],
                                  v_acc.rearrange("p (j t) -> p j t", t=512)
                                  [:, j, sub * 128:sub * 128 + 128],
                                  sb_idh, inc=(q == 3))
                    b.wait("vector", "P", pm)
                    b.ve("scalar_tensor_tensor", v_xt[:, jg * 512:(jg + 1) * 512],
                         pp[:], 0.5, v_xt[:, jg * 512:(jg + 1) * 512],
                         ALU.mult, ALU.add)
                    fin_vm[bk] = b.vdone()
                b.wait("sync", "V", b.cnt["V"])
                b.dma(outd[gq * 512 + sub * 128: gq * 512 + sub * 128 + 128, :],
                      v_xt, sem="T")
        b.wait("sync", "T", b.cnt["T"])

    b.start(None, None)
    flow()
    final_counts = dict(b.cnt)

    for engname in ENGINES_EMIT:
        def emit(eng, en=engname):
            b.start(en, eng)
            flow()
            assert b.cnt == final_counts, f"nondeterministic flow on {en}: {b.cnt} vs {final_counts}"
        getattr(blk, engname)(emit)

    stack.close()
    return nc


_NC_CACHE = {}


def kernel(**inputs):
    x = np.asarray(inputs["L0_attn_h"])
    wq = np.ascontiguousarray(inputs["L0_attn_wq"], np.float32)
    wk = np.ascontiguousarray(inputs["L0_attn_wk"], np.float32)
    wv = np.ascontiguousarray(inputs["L0_attn_wv"], np.float32)
    wo = np.ascontiguousarray(inputs["L0_attn_wo"], np.float32)
    bq = np.ascontiguousarray(inputs["L0_attn_bq"], np.float32).reshape(NH, DK)
    bk = np.ascontiguousarray(inputs["L0_attn_bk"], np.float32).reshape(NH, DK)
    bv = np.ascontiguousarray(inputs["L0_attn_bv"], np.float32)
    bo = np.ascontiguousarray(inputs["L0_attn_bo"], np.float32).reshape(NH, DK)
    lng = np.ascontiguousarray(inputs["L0_attn_ln_g"], np.float32)
    lnb = np.ascontiguousarray(inputs["L0_attn_ln_b"], np.float32)
    gp = np.ascontiguousarray(inputs["L0_ffn_gate_W_packed"], np.int32)
    gs = np.ascontiguousarray(inputs["L0_ffn_gate_scales"], np.float32)
    up = np.ascontiguousarray(inputs["L0_ffn_up_W_packed"], np.int32)
    us = np.ascontiguousarray(inputs["L0_ffn_up_scales"], np.float32)
    dp = np.ascontiguousarray(inputs["L0_ffn_down_W_packed"], np.int32)
    ds = np.ascontiguousarray(inputs["L0_ffn_down_scales"], np.float32)

    if "nc" not in _NC_CACHE:
        _NC_CACHE["nc"] = build_program()
    nc = _NC_CACHE["nc"]

    ident_b = np.eye(128).astype(ml_dtypes.bfloat16)
    ident_f = np.eye(128, dtype=np.float32)
    ident_h = (0.5 * np.eye(128)).astype(np.float32)
    R = np.zeros((128, 128), np.float32)
    for d in range(128):
        R[d, (d + 64) % 128] = 1.0
    rpermT = np.ascontiguousarray(R.T).astype(ml_dtypes.bfloat16)
    ones_f = np.ones((1, 128), np.float32)

    theta = 1.0 / 10000.0 ** (np.arange(0, 128, 2, dtype=np.float64) / 128.0)
    theta2 = np.concatenate([theta, theta])
    signv = np.where(np.arange(128) < 64, -1.0, 1.0)
    ang_k = theta2[:, None] * np.arange(S, dtype=np.float64)[None, :]
    cosk_h = np.ascontiguousarray(np.cos(ang_k), np.float32)
    sink_h = np.ascontiguousarray(np.sin(ang_k) * signv[:, None], np.float32)
    kidx_h = np.ascontiguousarray(
        np.broadcast_to(np.arange(S, dtype=np.float32)[None, :], (128, S)))

    in_maps = []
    for c in range(8):
        seq = c // 4
        r = c % 4
        own_tiles = [4 * i + r for i in range(NT)]
        qpos_tl = np.stack([
            np.arange(g * 128, (g + 1) * 128, dtype=np.float32) for g in own_tiles])
        qpos = qpos_tl.reshape(-1).astype(np.float64)
        ang_q = theta2[:, None] * qpos[None, :]
        qcos_h = np.ascontiguousarray(np.cos(ang_q) * INVSQ, np.float32)
        qsin_h = np.ascontiguousarray(np.sin(ang_q) * signv[:, None] * INVSQ,
                                      np.float32)
        qsel = np.zeros((128, 4), np.float32)
        qsel[:, r] = 1.0
        in_maps.append(dict(
            x0=np.ascontiguousarray(x[seq], np.float32),
            wq=wq, wk=wk, wv=wv, wo=wo,
            bqt=bq, bkt=bk, bot=bo,
            bv_row=bv.reshape(1, D),
            lngt=lng.reshape(NH, DK), lnbt=lnb.reshape(NH, DK),
            gp=gp, up=up, dp=dp, gs=gs, us=us, ds=ds,
            qpos_t=qpos_tl,
            cosk=cosk_h, sink=sink_h, kidxd=kidx_h,
            qcosd=qcos_h, qsind=qsin_h,
            epscol=np.full((128, 1), EPS, np.float32),
            ident_b=ident_b, ident_f=ident_f, ident_h=ident_h,
            rpermT=rpermT, ones_f=ones_f,
            ones_b=np.ones((1, 128), ml_dtypes.bfloat16), qsel=qsel,
        ))

    res = run_bass_kernel_spmd(nc, in_maps, core_ids=list(range(8)))

    final = np.empty((2, S, D), np.float32)
    for c in range(8):
        seq = c // 4
        r = c % 4
        o = np.asarray(res.results[c]["out"], np.float32)
        for i in range(NT):
            g = 4 * i + r
            final[seq, g * 128:(g + 1) * 128, :] = o[i * 128:(i + 1) * 128, :]
    return final
